# version 21
# baseline (speedup 1.0000x reference)
import math
import contextlib
import numpy as np
import ml_dtypes
import concourse.bass as bass
import concourse.mybir as mybir
from concourse.bass_utils import run_bass_kernel_spmd

F32 = mybir.dt.float32
BF16 = mybir.dt.bfloat16
AL = mybir.AluOpType
AF = mybir.ActivationFunctionType
AX = mybir.AxisListType.X

B, S, D = 16, 2048, 1024
NCORES = 8
BPC = B // NCORES
NH, DH, DHR, DL = 16, 64, 32, 256
NE, EF = 32, 512
HG = 4
NG = NH // HG
NT = S // 128
SCALE = 1.0 / float(np.float32(np.sqrt(np.float32(96.0))))
SC_HI = float(np.float32(SCALE))
SC_LO = float(np.float32(SCALE - SC_HI))
CLIP_HI = 80.0 * SCALE
KB = 1024

COMPUTE_ENGINES = ("tensor", "vector", "scalar", "gpsimd", "sync")
QUEUES = {"qsync": "sync", "qsc": "scalar", "qgp": "gpsimd"}
STREAMS = COMPUTE_ENGINES + tuple(QUEUES)


class Prog:

    def __init__(self):
        self.issue = {s: [] for s in STREAMS}
        self.cstreams = {}
        self.writer = {}
        self.readers = {}
        self.conflicts = {}
        self.name_keys = {}
        self.active_names = set()
        self.order = 0

    def set_conflicts(self, table):
        self.conflicts = table

    def add(self, eng, emit, reads=(), writes=(), queue=None, must_inc=False,
            small=False, slot=None, tag=""):
        issue_stream = queue if queue is not None else eng
        if queue is not None:
            must_inc = True
            if slot is None:
                b0 = writes[0] if writes else ("_",)
                slot = "_".join(str(x) for x in b0) if isinstance(b0, tuple) \
                    else str(b0)
            cstream = (queue, slot)
        else:
            cstream = eng
        cidx = self.cstreams.get(cstream, 0)
        self.cstreams[cstream] = cidx + 1
        waits = {}

        def dep(pcs, pci, raw=False):
            if pcs == cstream and not isinstance(cstream, tuple) \
                    and cstream not in ("vector", "scalar", "gpsimd"):
                return
            if queue is not None and pcs == eng and raw \
                    and eng != "gpsimd":
                return
            if waits.get(pcs, -1) < pci:
                waits[pcs] = pci

        for b in reads:
            w = self.writer.get(b)
            if w is not None:
                dep(w[0], w[1], raw=True)
        for b in writes:
            for (rs, ri) in self.readers.get(b, ()):
                dep(rs, ri)
            w = self.writer.get(b)
            if w is not None:
                dep(w[0], w[1])
        wnames = {b[0] if isinstance(b, tuple) else b for b in writes}
        for n in wnames:
            conf = self.conflicts.get(n)
            if not conf:
                continue
            for other in conf & self.active_names:
                for k in self.name_keys.get(other, ()):
                    w = self.writer.get(k)
                    if w is not None:
                        dep(w[0], w[1])
                    for (rs, ri) in self.readers.get(k, ()):
                        dep(rs, ri)

        op = dict(eng=eng, emit=emit, waits=waits,
                  queue=queue, cstream=cstream, cidx=cidx, inc=None,
                  must_inc=must_inc, order=self.order, tag=tag)
        self.order += 1
        self.issue[issue_stream].append(op)
        for b in writes:
            self.writer[b] = (cstream, cidx, small)
            self.readers[b] = []
            n = b[0] if isinstance(b, tuple) else b
            self.name_keys.setdefault(n, set()).add(b)
            self.active_names.add(n)
        for b in reads:
            self.readers.setdefault(b, []).append((cstream, cidx))
        return (cstream, cidx)

    def finalize_and_emit(self, nc):
        needed = set()
        allops = [op for s in STREAMS for op in self.issue[s]]
        for op in allops:
            for pcs, pci in op["waits"].items():
                needed.add((pcs, pci))
        inc_no = {}
        counts = {}
        for s in STREAMS:
            for op in self.issue[s]:
                cs, ci = op["cstream"], op["cidx"]
                if (cs, ci) in needed or op["must_inc"]:
                    op["_inc_pending"] = True
        by_cs = {}
        for op in allops:
            by_cs.setdefault(op["cstream"], []).append(op)
        totals = {}
        for cs, ops in by_cs.items():
            ops.sort(key=lambda o: o["cidx"])
            cnt = 0
            for op in ops:
                if op.get("_inc_pending"):
                    cnt += 1
                    inc_no[(cs, op["cidx"])] = cnt
                    op["inc"] = cnt
            totals[cs] = cnt

        def unit(cs):
            return 16 if isinstance(cs, tuple) else 1

        es = contextlib.ExitStack()
        with es:
            used = [cs for cs in by_cs if totals.get(cs, 0) > 0]
            sems = {cs: es.enter_context(nc.semaphore(
                ("sem_" + "_".join(cs)) if isinstance(cs, tuple)
                else "sem_" + cs)) for cs in used}
            block = es.enter_context(nc.Block())

            def make_closure(engine_name):
                def body(eng):
                    items = []
                    for s in STREAMS:
                        if (s == engine_name and s in COMPUTE_ENGINES) or \
                           (s in QUEUES and QUEUES[s] == engine_name):
                            for op in self.issue[s]:
                                items.append((op["order"], op))
                    items.sort(key=lambda t: t[0])
                    waited = {}
                    for _, op in items:
                        for pcs in sorted(op["waits"],
                                          key=lambda c: str(c)):
                            pci = op["waits"][pcs]
                            val = inc_no[(pcs, pci)] * unit(pcs)
                            if waited.get(pcs, -1) >= val:
                                continue
                            eng.wait_ge(sems[pcs], val)
                            waited[pcs] = val
                        inst = op["emit"](eng)
                        if op["inc"] is not None:
                            inst.then_inc(sems[op["cstream"]],
                                          unit(op["cstream"]))
                    if engine_name == "gpsimd":
                        for cs in used:
                            if isinstance(cs, tuple):
                                eng.wait_ge(sems[cs], totals[cs] * unit(cs))
                return body

            block.tensor(make_closure("tensor"))
            block.vector(make_closure("vector"))
            block.scalar(make_closure("scalar"))
            block.gpsimd(make_closure("gpsimd"))
            block.sync(make_closure("sync"))
        return nc


def _rope_tables():
    import jax
    import jax.numpy as jnp
    with jax.default_device(jax.devices("cpu")[0]):
        inv = 10000.0 ** (-jnp.arange(0, DHR, 2, dtype=jnp.float32) / DHR)
        ang = jnp.arange(S, dtype=jnp.float32)[:, None] * inv
        cos = np.asarray(jnp.cos(ang))
        sin = np.asarray(jnp.sin(ang))

    def tok_major(tab):
        return tab.reshape(NT, 128, 16).transpose(1, 0, 2).reshape(
            128, NT * 16).astype(np.float32)

    return np.ascontiguousarray(
        np.concatenate([tok_major(cos), tok_major(sin)], axis=1))


def _tri_tile():
    i = np.arange(128)[:, None]
    j = np.arange(512)[None, :]
    return np.ascontiguousarray(
        np.where((j >= i) | (j >= 128), CLIP_HI, -1e30).astype(np.float32))


def _perm_qr(w):
    cols = []
    nh = w.shape[1] // DHR
    for h in range(nh):
        base = h * DHR
        cols.extend(base + 2 * np.arange(16))
        cols.extend(base + 2 * np.arange(16) + 1)
    return np.ascontiguousarray(np.asarray(w)[:, np.array(cols)])


def _rope(p, src, dst, cos4, bufs, nh, rkeys, wkeys, tag, trange=None):
    t0, t1 = trange if trange is not None else (0, NT)
    nt = t1 - t0
    s4 = src.ap().rearrange("p (t h s d) -> p t h s d", h=nh, s=2, d=16)[
        :, t0:t1]
    d4 = dst.ap().rearrange("p (t h s d) -> p t h s d", h=nh, s=2, d=16)[
        :, t0:t1]
    x1, x2 = s4[:, :, :, 0, :], s4[:, :, :, 1, :]
    r1, r2 = d4[:, :, :, 0, :], d4[:, :, :, 1, :]
    n = nt * nh * 16
    rA, rB, rC = bufs
    kA, kB = ("ropeA",), ("ropeB",)

    def t3(buf):
        return buf.ap()[:, 0:n].rearrange("p (t h d) -> p t h d", h=nh, d=16)

    def ctab(i):
        a = cos4.ap()[:, i * 256 + t0 * 16:i * 256 + t1 * 16].rearrange(
            "p (t o d) -> p t o d", o=1, d=16)
        return a.broadcast_to((128, nt, nh, 16))

    ch, sh = ctab(0), ctab(1)
    ck = [("cos4",)]
    A, Bv = t3(rA), t3(rB)

    def tt(out, i0, i1, op, rd, wr, tg):
        p.add("vector", lambda e: e.tensor_tensor(out, i0, i1, op),
              reads=rd, writes=wr, tag=tag + tg)

    tt(A, x1, ch, AL.mult, rkeys + ck, [kA], "p1")
    tt(Bv, x2, sh, AL.mult, rkeys + ck, [kB], "p2")
    tt(r1, A, Bv, AL.subtract, [kA, kB], wkeys, "w1")
    tt(A, x1, sh, AL.mult, rkeys + ck, [kA], "p3")
    tt(Bv, x2, ch, AL.mult, rkeys + ck, [kB], "p4")
    tt(r2, A, Bv, AL.add, [kA, kB], wkeys, "w2")


_SEQ_LIMIT = None


def build_program():
    nc = bass.Bass("TRN2", target_bir_lowering=False)
    p = Prog()

    x_d = nc.dram_tensor("x", [BPC, S, D], F32, kind="ExternalInput").ap()
    wdq_d = nc.dram_tensor("wdq", [D, DL], F32, kind="ExternalInput").ap()
    wuq_d = nc.dram_tensor("wuq", [DL, D], F32, kind="ExternalInput").ap()
    wqr_d = nc.dram_tensor("wqr", [DL, NH * 32], F32, kind="ExternalInput").ap()
    wdkv_d = nc.dram_tensor("wdkv", [D, DL], F32, kind="ExternalInput").ap()
    wuk_d = nc.dram_tensor("wuk", [DL, D], F32, kind="ExternalInput").ap()
    wuv_d = nc.dram_tensor("wuv", [DL, D], F32, kind="ExternalInput").ap()
    wkr_d = nc.dram_tensor("wkr", [D, 32], F32, kind="ExternalInput").ap()
    wo_d = nc.dram_tensor("wo", [D, D], F32, kind="ExternalInput").ap()
    selT_d = nc.dram_tensor("selT", [D, NE], F32, kind="ExternalInput").ap()
    keys_d = nc.dram_tensor("keys", [NE, D, EF], F32, kind="ExternalInput").ap()
    vals_d = nc.dram_tensor("vals", [NE, EF, D], F32, kind="ExternalInput").ap()
    y_d = nc.dram_tensor("y", [BPC, S, D], F32, kind="ExternalOutput").ap()
    kvscr = nc.dram_tensor("kvscr", [NE, 8, 128, 1024], BF16,
                           kind="Internal").ap()
    wscr = nc.dram_tensor("wscr", [128, 19712], BF16, kind="Internal").ap()

    cos_c = nc.inline_tensor(_rope_tables(), name="cos4c")
    tri_c = nc.inline_tensor(_tri_tile(), name="tric")
    id_c = nc.inline_tensor(
        np.eye(128, dtype=np.float32).astype(ml_dtypes.bfloat16), name="identc")
    idf_c = nc.inline_tensor(np.eye(128, dtype=np.float32), name="identfc")

    def sb(name, shape, dtype):
        return nc.alloc_sbuf_tensor(name, shape, dtype)

    xt_fm = sb("xt_fm", [128, 8 * S], BF16)
    attn_fm = sb("attn_fm", [128, 8 * S], BF16)
    stats = [sb(f"stats{i}", [128, 16], F32) for i in range(2)]
    smt = [sb(f"smt{i}", [128, 8], F32) for i in range(2)]
    eps_t = sb("eps_t", [128, 1], F32)
    cos4 = sb("cos4", [128, 2 * NT * 16], F32)
    tri = sb("tri", [128, 512], F32)
    ident = sb("ident", [128, 128], BF16)
    identf = sb("identf", [128, 128], F32)
    ones64 = sb("ones64", [1, 64], BF16)
    ones128 = sb("ones128", [64, 128], BF16)
    selT_bf = sb("selT_bf", [128, 8 * NE], BF16)
    gates_all = sb("gates_all", [128, NT * NE], F32)
    g_work = sb("g_work", [128, NT * NE], F32)
    g_ind = sb("g_ind", [128, NT * NE], F32)
    g_m = sb("g_m", [128, 4 * NT], F32)
    gate_full = sb("gate_full", [128, NT * NE], F32)
    E_T = [sb(f"E_T{i}", [128, 512], BF16) for i in range(2)]
    S_scr = [sb(f"S_scr{i}", [128, 512], F32) for i in range(2)]
    rs_f = [sb(f"rs_f{i}", [1, 512], BF16) for i in range(2)]
    rs_bf = [sb(f"rs_bf{i}", [1, 512], BF16) for i in range(2)]
    attn_tmp = [sb(f"attn_tmp{i}", [64, 512], BF16) for i in range(2)]
    lnscr = sb("lnscr", [128, D], F32)
    castbf = sb("castbf", [128, D], BF16)

    OV = 109 * KB
    arena = nc.alloc_sbuf_tensor("arena", [128, OV], mybir.dt.uint8)
    base = nc.lookup_mloc(arena).addr

    layout = {
        "xp":      (0 * KB, [128, NT * D], F32),
        "land0":   (64 * KB, [128, D], F32),
        "keysb0":  (68 * KB, [128, 8 * EF], BF16),
        "valsb0":  (76 * KB, [128, 4 * D], BF16),
        "keysb1":  (84 * KB, [128, 8 * EF], BF16),
        "valsb1":  (92 * KB, [128, 4 * D], BF16),
        "h_sb0":   (100 * KB, [128, 4 * EF], BF16),
        "h_sb1":   (104 * KB, [128, 4 * EF], BF16),
        "c_q":     (0 * KB, [128, 2 * S], BF16),
        "c_kv":    (8 * KB, [128, 2 * S], BF16),
        "wdq":     (16 * KB, [128, 8 * DL], BF16),
        "wdkv":    (20 * KB, [128, 8 * DL], BF16),
        "wuq":     (24 * KB, [128, 2 * D], BF16),
        "wuk":     (28 * KB, [128, 2 * D], BF16),
        "wuv":     (32 * KB, [128, 2 * D], BF16),
        "wqr":     (36 * KB, [128, 2 * 512], BF16),
        "wkr":     (38 * KB, [128, 8 * 32], BF16),
        "kr_all":  (38 * KB + 512, [128, NT * 32], F32),
        "krall_bf": (40 * KB + 512, [128, NT * 32], BF16),
        "Krst":    (41 * KB + 512, [32, S], BF16),
        "Qst":     (45 * KB + 512, [96, HG * S], BF16),
        "Kst":     (61 * KB + 512, [96, HG * S], BF16),
        "V_g":     (77 * KB + 512, [128, NT * HG * 65], BF16),
        "qr_all":  (86 * KB, [128, NT * 128], F32),
        "qrall_bf": (94 * KB, [128, NT * 128], BF16),
        "S_scr3":  (94 * KB, [128, 512], F32),
        "E_T3":    (96 * KB, [128, 512], BF16),
        "E_T2":    (98 * KB, [128, 512], BF16),
        "S_scr2":  (99 * KB, [128, 512], F32),
        "ropeA":   (101 * KB, [128, 1024], F32),
        "ropeB":   (105 * KB, [128, 1024], F32),
        "ropeC":   (77 * KB + 512, [128, 1024], F32),
        "cvl0":    (0 * KB, [128, D], F32),
        "cvl1":    (4 * KB, [128, D], F32),
        "cvl2":    (8 * KB, [128, D], F32),
        "cvl3":    (12 * KB, [128, D], F32),
        "cvb0":    (16 * KB, [128, D], BF16),
        "cvb1":    (18 * KB, [128, D], BF16),
        "wo":      (84 * KB, [128, 8 * D], BF16),
        "xrot0":   (100 * KB, [128, D], F32),
        "xrot1":   (104 * KB, [128, D], F32),
    }
    ov = {}
    iv = {}
    for nm, (off, shape, dt) in layout.items():
        nbytes = shape[1] * (2 if dt == BF16 else (1 if dt == mybir.dt.uint8 else 4))
        iv[nm] = (off, off + nbytes)
        assert off + nbytes <= OV, (nm, off, nbytes)
        ov[nm] = nc.alloc_sbuf_tensor_at(nm, shape, dt, offset=base + off)
    conf = {nm: set() for nm in layout}
    names = list(layout)
    for a in names:
        for b in names:
            if a >= b:
                continue
            (a0, a1), (b0, b1) = iv[a], iv[b]
            if a0 < b1 and b0 < a1:
                conf[a].add(b)
                conf[b].add(a)
    p.set_conflicts(conf)

    E_T.append(ov["E_T2"])
    E_T.append(ov["E_T3"])
    S_scr.append(ov["S_scr2"])
    S_scr.append(ov["S_scr3"])
    xrot = [ov["xrot0"], ov["xrot1"]]

    pb = []
    pb_bf = None
    for i in range(8):
        if i == 5:
            pb_bf = nc.alloc_psum_tensor("pbbf", [128, 1024], BF16)
            pb.append(pb_bf)
        else:
            pb.append(nc.alloc_psum_tensor(f"pb{i}", [128, 512], F32))

    b5s = {"t": 0, "engs": ()}

    def b5_begin():
        return [("b5m", b5s["t"], e) for e in b5s["engs"]]

    def b5_end(engs):
        b5s["t"] += 1
        b5s["engs"] = tuple(engs)
        return {e: ("b5m", b5s["t"], e) for e in engs}

    p.add("sync", lambda e: e.dma_start(out=cos4[:], in_=cos_c.ap()[:, :]),
          writes=[("cos4",)], queue="qsync")
    p.add("sync", lambda e: e.dma_start(out=tri[:], in_=tri_c.ap()[:, :]),
          writes=[("tri",)], queue="qsync")
    p.add("sync", lambda e: e.dma_start(out=ident[:], in_=id_c.ap()[:, :]),
          writes=[("ident",)], queue="qsync")
    p.add("sync", lambda e: e.dma_start(out=identf[:], in_=idf_c.ap()[:, :]),
          writes=[("identf",)], queue="qsync")
    p.add("vector", lambda e: e.memset(ones64[:], 1.0), writes=[("ones64",)])
    p.add("vector", lambda e: e.memset(ones128[:], 1.0), writes=[("ones128",)])
    p.add("vector", lambda e: e.memset(eps_t[:], 1e-5), writes=[("eps_t",)])
    selT_r = selT_d.rearrange("(c p) e -> p c e", p=128)
    p.add("sync", lambda e: e.dma_start(out=ov["land0"][:, 0:256], in_=selT_r),
          writes=[("land0",)], queue="qsync")
    p.add("vector", lambda e: e.tensor_copy(selT_bf[:], ov["land0"][:, 0:256]),
          reads=[("land0",)], writes=[("selT_bf",)])

    wsrc = [
        ("wdq", wdq_d.rearrange("(c p) l -> p c l", p=128), 8, DL),
        ("wdkv", wdkv_d.rearrange("(c p) l -> p c l", p=128), 8, DL),
        ("wuq", wuq_d.rearrange("(c p) n -> p c n", p=128), 2, D),
        ("wuk", wuk_d.rearrange("(c p) n -> p c n", p=128), 2, D),
        ("wuv", wuv_d.rearrange("(c p) n -> p c n", p=128), 2, D),
        ("wqr", wqr_d.rearrange("(c p) n -> p c n", p=128), 2, 512),
        ("wkr", wkr_d.rearrange("(c p) n -> p c n", p=128), 8, 32),
        ("wo", wo_d.rearrange("(c p) n -> p c n", p=128), 8, D),
    ]
    wbase = {}
    wnpieces = {}
    conv_pieces = []
    col0 = 0
    pcnt = [0]
    for wname, src3, nch, n in wsrc:
        wbase[wname] = col0
        per = max(1, 1024 // n)
        np_ = 0
        for i0 in range(0, nch, per):
            k = min(per, nch - i0)
            cols = k * n
            dst0 = col0 + i0 * n
            i = pcnt[0]
            pcnt[0] += 1
            lname = f"cvl{i % 4}"
            bname = f"cvb{i % 2}"
            ceng = "vector" if i % 2 == 0 else "scalar"

            def emit_piece(src3=src3, i0=i0, k=k, cols=cols, dst0=dst0,
                           lname=lname, bname=bname, ceng=ceng,
                           wname=wname, np_=np_):
                p.add("sync", lambda e: e.dma_start(
                    out=ov[lname][:, 0:cols], in_=src3[:, i0:i0 + k, :]),
                    writes=[(lname,)], queue="qsync", slot=lname,
                    tag="wcin_" + wname)

                def _cast(e):
                    f = e.tensor_copy if ceng == "vector" else e.copy
                    return f(ov[bname][:, 0:cols], ov[lname][:, 0:cols])
                p.add(ceng, _cast, reads=[(lname,)], writes=[(bname,)],
                      tag="wcc_" + wname)
                p.add("gpsimd", lambda e: e.dma_start(
                    out=wscr[:, dst0:dst0 + cols], in_=ov[bname][:, 0:cols]),
                    reads=[(bname,)], writes=[("wscr", wname, np_)],
                    queue="qgp", slot=bname, tag="wcout_" + wname)
            conv_pieces.append(emit_piece)
            np_ += 1
        wnpieces[wname] = np_
        col0 += nch * n
    assert col0 == 19712

    def load_weights_dma(names, tagname):
        for wname in names:
            cols = sum(nch * n for (w, _, nch, n) in wsrc if w == wname)
            p.add("sync", lambda e, wname=wname, cols=cols: e.dma_start(
                out=ov[wname][:, 0:cols],
                in_=wscr[:, wbase[wname]:wbase[wname] + cols]),
                reads=[("wscr", wname, i) for i in range(wnpieces[wname])],
                writes=[(wname,)], queue="qsync", slot="wld_" + wname,
                tag="wload_" + tagname)

    def _lnbufs(par):
        if par == 0:
            return ((S_scr[0], 0, [("S_scr0", 0), ("S_scr0", 1)]),
                    (S_scr[1], 0, [("S_scr1", 0), ("S_scr1", 1)]))
        if par == 1:
            return ((lnscr, 0, [("lnscr", 0)]),
                    (lnscr, 512, [("lnscr", 1)]))
        return ((ov["lnD2"], 0, [("lnD2", 0)]),
                (ov["lnD2"], 512, [("lnD2", 1)]))

    def _castbufs(par):
        if par == 0:
            return ((E_T[0], 0, ("E_T0",)),
                    (E_T[1], 0, ("E_T1",)))
        if par == 1:
            return ((castbf, 0, ("castbf", 0)),
                    (castbf, 512, ("castbf", 1)))
        return ((ov["castD2"], 0, ("castD2", 0)),
                (ov["castD2"], 512, ("castD2", 1)))

    def ln_stats(srcA, srcB, skA, skB, t, tag, par=None):
        par = t % 2 if par is None else par
        st = stats[par]
        p.add("vector", lambda e: e.bn_stats(st[:, 0:6], srcA),
              reads=skA, writes=[("stats", par, 0)], small=True,
              tag=tag + "bn0")
        p.add("vector", lambda e: e.bn_stats(st[:, 6:12], srcB),
              reads=skB, writes=[("stats", par, 1)], small=True,
              tag=tag + "bn1")
        p.add("vector", lambda e: e.bn_aggr(st[:, 12:14], st[:, 0:12]),
              reads=[("stats", par, 0), ("stats", par, 1)],
              writes=[("stats", par, 2)], small=True, tag=tag + "aggr")

    def ln_norm(srcA, srcB, skA, skB, t, tag, par=None):
        par = t % 2 if par is None else par
        st = stats[par]
        sm = smt[par]
        lb = _lnbufs(par)
        p.add("scalar", lambda e: e.activation(sm[:, 0:1], st[:, 13:14],
                                               AF.Ln, bias=eps_t.ap()),
              reads=[("stats", par, 2), ("eps_t",)], writes=[("smt", par, 0)],
              small=True, tag=tag + "sd")
        p.add("scalar", lambda e: e.activation(sm[:, 1:2], sm[:, 0:1],
                                               AF.Exp, scale=-0.5),
              reads=[("smt", par, 0)], writes=[("smt", par, 1)], small=True,
              tag=tag + "rstd")
        p.add("vector", lambda e: e.scalar_tensor_tensor(
            sm[:, 2:3], st[:, 12:13], -1.0, sm[:, 1:2], AL.mult, AL.mult),
            reads=[("stats", par, 2), ("smt", par, 1)],
            writes=[("smt", par, 2)], small=True, tag=tag + "nmr")
        for hf, (src, sk) in enumerate(((srcA, skA), (srcB, skB))):
            lt, lo, lk = lb[hf]
            p.add("scalar", lambda e, src=src, lt=lt, lo=lo: e.activation(
                lt.ap()[:, lo:lo + 512], src, AF.Identity,
                bias=sm.ap()[:, 2:3], scale=sm.ap()[:, 1:2]),
                reads=list(sk) + [("smt", par, 1), ("smt", par, 2)],
                writes=lk, tag=tag + f"ln{hf}")

    def ln_stage2(t, tag, xtn, xtb, part=None, par=None):
        par = t % 2 if par is None else par
        lb = _lnbufs(par)
        cb = _castbufs(par)
        if part in (None, "tr"):
            for hf in range(2):
                lt, lo, lk = lb[hf]
                ct, co, ck = cb[hf]
                p.add("scalar", lambda e, lt=lt, lo=lo, ct=ct, co=co: e.copy(
                    ct.ap()[:, co:co + 512], lt.ap()[:, lo:lo + 512]),
                    reads=lk, writes=[ck], tag=tag + f"cast{hf}")
            trd = b5_begin()
            for i in range(8):
                ct, co, ck = cb[i // 4]
                c0 = co + (i % 4) * 128
                p.add("tensor", lambda e, i=i, ct=ct, c0=c0: e.transpose(
                    pb_bf.ap()[:, i * 128:(i + 1) * 128],
                    ct.ap()[:, c0:c0 + 128], ident[:]),
                    reads=[ck, ("ident",)] + (trd if i == 0 else []),
                    writes=[("pb", 5, i)], tag=tag + f"tr{i}")
            if part == "tr":
                return
        mk = b5_end(("vector", "scalar"))
        xt3 = xtb.ap().rearrange("p (c n) -> p c n", n=S)
        pb3 = pb_bf.ap().rearrange("p (c n) -> p c n", n=128)
        p.add("vector", lambda e: e.tensor_copy(
            xt3[:, 0:4, t * 128:(t + 1) * 128], pb3[:, 0:4, :]),
            reads=[("pb", 5, i) for i in range(8)],
            writes=[(xtn, t, "lo"), mk["vector"]] +
            [(xtn, h, t // 4) for h in range(8)],
            tag=tag + "evlo")
        p.add("scalar", lambda e: e.copy(
            xt3[:, 4:8, t * 128:(t + 1) * 128], pb3[:, 4:8, :]),
            reads=[("pb", 5, i) for i in range(8)] + [(xtn, t, "lo")],
            writes=[(xtn, t), mk["scalar"]] +
            [(xtn, h, t // 4) for h in range(8, NH)],
            tag=tag + "evhi")

    def xtna(seq):
        return ("xt_fm", xt_fm) if seq % 2 == 0 else ("attn_fm", attn_fm)

    def afna(seq):
        return ("attn_fm", attn_fm) if seq % 2 == 0 else ("xt_fm", xt_fm)

    def emit_A_chunks(seq, woven):
        sq = f"s{seq}_"
        xtn, xtb = xtna(seq)
        per_t = []
        for t in range(NT):
            par = t % 2
            if not woven:
                xr = xrot[par]
                xk = (f"xrot{par}",)

                def c1(t=t, xr=xr, xk=xk):
                    p.add("sync", lambda e: e.dma_start(
                        out=xr[:], in_=x_d[seq][t * 128:(t + 1) * 128, :]),
                        writes=[xk], queue="qsync", slot=xk[0],
                        tag=sq + f"xload{t}")
                srcA = xr.ap()[:, 0:512]
                srcB = xr.ap()[:, 512:1024]
                skA = skB = [xk]
            else:
                def c1(t=t):
                    p.add("scalar", lambda e: e.dma_start(
                        out=g_work[:, 0:512],
                        in_=x_d[seq][t * 128:(t + 1) * 128, 0:512]),
                        writes=[("g_work",)], queue="qsc", slot="a1xa",
                        tag=sq + f"xload{t}a")
                    p.add("scalar", lambda e: e.dma_start(
                        out=g_ind[:, 0:512],
                        in_=x_d[seq][t * 128:(t + 1) * 128, 512:1024]),
                        writes=[("g_ind",)], queue="qsc", slot="a1xb",
                        tag=sq + f"xload{t}b")
                srcA = g_work.ap()[:, 0:512]
                srcB = g_ind.ap()[:, 0:512]
                skA = [("g_work",)]
                skB = [("g_ind",)]

            pr = t % 2

            def c2(t=t, srcA=srcA, srcB=srcB, skA=skA, skB=skB, pr=pr):
                ln_stats(srcA, srcB, skA, skB, t, sq + f"A{t}_", par=pr)

            def c3(t=t, srcA=srcA, srcB=srcB, skA=skA, skB=skB, pr=pr):
                ln_norm(srcA, srcB, skA, skB, t, sq + f"A{t}_", par=pr)

            def c4a(t=t, pr=pr):
                ln_stage2(t, sq + f"A{t}_", xtn, xtb,
                          part="tr" if woven else None, par=pr)

            def c4b(t=t, pr=pr):
                if woven:
                    ln_stage2(t, sq + f"A{t}_", xtn, xtb, part="ev", par=pr)
            per_t.append((c1, c2, c3, c4a, c4b))
        order = [per_t[0][0], per_t[0][1], per_t[0][2]]
        for t in range(1, NT):
            order += [per_t[t][0], per_t[t - 1][3], per_t[t][1],
                      per_t[t - 1][4], per_t[t][2]]
        order += [per_t[NT - 1][3], per_t[NT - 1][4]]
        return order

    nseq = BPC if _SEQ_LIMIT is None else _SEQ_LIMIT
    woven = set()
    for seq in range(nseq):
        sq = f"s{seq}_"
        xtn, xtb = xtna(seq)
        afn, afb = afna(seq)
        if seq not in woven:
            if seq == 0:
                for _ in range(4):
                    conv_pieces.pop(0)()
            for ch in emit_A_chunks(seq, woven=False):
                ch()
                if conv_pieces:
                    conv_pieces.pop(0)()
            while conv_pieces:
                conv_pieces.pop(0)()

        load_weights_dma(("wdq", "wdkv", "wuq", "wuk", "wuv", "wqr", "wkr"),
                         sq + "aw")

        for (wname, cname) in (("wdq", "c_q"), ("wdkv", "c_kv")):
            for lc in range(2):
                for qg in range(4):
                    bank = qg % 2
                    xreads = [(xtn, qg * 4 + i) for i in range(4)]
                    for xc in range(8):
                        p.add("tensor",
                              lambda e, lc=lc, qg=qg, xc=xc, wname=wname,
                              bank=bank, xtb=xtb:
                              e.matmul(pb[bank][:, :],
                                       ov[wname][:, xc * DL + lc * 128:
                                                 xc * DL + lc * 128 + 128],
                                       xtb[:, xc * S + qg * 512:
                                           xc * S + (qg + 1) * 512],
                                       start=(xc == 0), stop=(xc == 7)),
                              reads=[(wname,)] + xreads,
                              writes=[("pb", bank)], tag=sq + cname + "mm")
                    p.add("vector", lambda e, lc=lc, qg=qg, cname=cname,
                          bank=bank:
                          e.tensor_copy(ov[cname][:, lc * S + qg * 512:
                                                  lc * S + (qg + 1) * 512],
                                        pb[bank][:, :]),
                          reads=[("pb", bank)], writes=[(cname, lc, qg)],
                          tag=sq + cname + "ev")
        for t in range(NT):
            qb = 4 if t % 2 == 0 else 6
            for xc in range(8):
                p.add("tensor", lambda e, t=t, xc=xc, qb=qb, xtb=xtb: e.matmul(
                    pb[qb][:, 0:32],
                    xtb[:, xc * S + t * 128:xc * S + (t + 1) * 128],
                    ov["wkr"][:, xc * 32:(xc + 1) * 32],
                    start=(xc == 0), stop=(xc == 7)),
                    reads=[(xtn, t), ("wkr",)], writes=[("pb", qb)],
                    tag=sq + "krmm")
            p.add("vector", lambda e, t=t, qb=qb: e.tensor_copy(
                ov["kr_all"][:, t * 32:(t + 1) * 32], pb[qb][:, 0:32]),
                reads=[("pb", qb)], writes=[("kr_all", t)], tag=sq + "krev")
        for hf in range(2):
            _rope(p, ov["kr_all"], ov["krall_bf"], cos4,
                  (ov["ropeA"], ov["ropeB"], ov["ropeC"]), 1,
                  [("kr_all", t) for t in range(hf * 8, hf * 8 + 8)],
                  [("krall_bf", hf)], sq + f"krr{hf}",
                  trange=(hf * 8, hf * 8 + 8))
            trd = b5_begin()
            for tl in range(8):
                t = hf * 8 + tl
                p.add("tensor", lambda e, t=t, tl=tl: e.transpose(
                    pb_bf.ap()[0:32, tl * 128:tl * 128 + 128],
                    ov["krall_bf"][:, t * 32:(t + 1) * 32], ident[:]),
                    reads=[("krall_bf", hf), ("ident",)] +
                    (trd if tl == 0 else []),
                    writes=[("pb", 5, tl)],
                    tag=sq + "krtr")
            mkv = b5_end(("vector",))
            for tl in range(8):
                t = hf * 8 + tl
                p.add("vector", lambda e, t=t, tl=tl: e.tensor_scalar(
                    ov["Krst"][0:32, t * 128:(t + 1) * 128],
                    pb_bf.ap()[0:32, tl * 128:tl * 128 + 128],
                    SC_HI, None, AL.mult),
                    reads=[("pb", 5, i) for i in range(8)] if tl == 0
                    else [("pb", 5, tl)],
                    writes=[("Krst", t)], tag=sq + "krtrev")
                p.add("vector", lambda e, t=t, tl=tl: e.scalar_tensor_tensor(
                    ov["Krst"][0:32, t * 128:(t + 1) * 128],
                    pb_bf.ap()[0:32, tl * 128:tl * 128 + 128], SC_LO,
                    ov["Krst"][0:32, t * 128:(t + 1) * 128], AL.mult, AL.add),
                    reads=[("pb", 5, tl), ("Krst", t)],
                    writes=[("Krst", t)] + ([mkv["vector"]] if tl == 7
                                            else []),
                    tag=sq + "krtrev2")

        V3 = ov["V_g"].ap().rearrange("p (t h c) -> p t h c", h=HG, c=65)
        p.add("vector", lambda e: e.memset(V3[:, :, :, 64:65], 1.0),
              writes=[("V_g", "ones")], tag=sq + "vones")

        pend_epi = [None]
        for g in range(NG):
            gq = f"{sq}g{g}_"
            if pend_epi[0] is not None:
                for blk in pend_epi[0]:
                    blk[0]()
                    blk[1]()
                pend_epi[0] = None
            for t in range(NT):
                qb = 4 if t % 2 == 0 else 6
                for lc in range(2):
                    p.add("tensor", lambda e, t=t, lc=lc, g=g, qb=qb: e.matmul(
                        pb[qb][:, 0:128],
                        ov["c_q"][:, lc * S + t * 128:lc * S + (t + 1) * 128],
                        ov["wqr"][:, lc * 512 + g * 128:
                                  lc * 512 + (g + 1) * 128],
                        start=(lc == 0), stop=(lc == 1)),
                        reads=[("c_q", lc, t // 4), ("wqr",)],
                        writes=[("pb", qb)], tag=gq + "qrmm")
                p.add("vector", lambda e, t=t, qb=qb: e.tensor_copy(
                    ov["qr_all"][:, t * 128:(t + 1) * 128], pb[qb][:, 0:128]),
                    reads=[("pb", qb)], writes=[("qr_all", t)], tag=gq + "qrev")
            for hf in range(2):
                _rope(p, ov["qr_all"], ov["qrall_bf"], cos4,
                      (ov["ropeA"], ov["ropeB"], ov["ropeC"]), HG,
                      [("qr_all", t) for t in range(hf * 8, hf * 8 + 8)],
                      [("qrall_bf", hf)], gq + f"qrr{hf}",
                      trange=(hf * 8, hf * 8 + 8))
            for j in range(HG):
                p.add("scalar", lambda e, j=j, g=g: e.dma_start(
                    out=ov["Kst"][64:96, j * S:(j + 1) * S],
                    in_=ov["Krst"][0:32, :]),
                    reads=[("Krst", t) for t in range(NT)],
                    writes=[("Kst", j, "r")], queue="qsc",
                    slot=f"krrep{j}", tag=gq + "krrep")
            for (wname, dname, cn) in (("wuq", "Qst", "c_q"),
                                       ("wuk", "Kst", "c_kv")):
                for ch2 in range(2):
                    for qg in range(4):
                        bank = qg % 2
                        for lc in range(2):
                            p.add("tensor",
                                  lambda e, wname=wname, ch2=ch2, qg=qg,
                                  lc=lc, bank=bank, g=g, cn=cn:
                                  e.matmul(pb[bank][:, :],
                                           ov[wname][:, lc * D + g * 256 +
                                                     ch2 * 128:
                                                     lc * D + g * 256 +
                                                     ch2 * 128 + 128],
                                           ov[cn][:, lc * S + qg * 512:
                                                  lc * S + (qg + 1) * 512],
                                           start=(lc == 0), stop=(lc == 1)),
                                  reads=[(wname,), (cn, lc, qg)],
                                  writes=[("pb", bank)], tag=gq + dname + "mm")
                        for jh in range(2):
                            j = 2 * ch2 + jh
                            dst = ov[dname][0:64, j * S + qg * 512:
                                            j * S + (qg + 1) * 512]
                            pin = pb[bank][jh * 64:jh * 64 + 64, :]
                            if dname == "Qst":
                                def _ev(e, dst=dst, pin=pin, jh=jh):
                                    f = e.tensor_copy if jh == 0 else e.copy
                                    return f(dst, pin)
                                p.add("vector" if jh == 0 else "scalar", _ev,
                                      reads=[("pb", bank)],
                                      writes=[(dname, j, "c", qg)],
                                      tag=gq + dname + "ev")
                            else:
                                p.add("vector", lambda e, dst=dst, pin=pin:
                                      e.tensor_scalar(dst, pin, SC_HI, None,
                                                      AL.mult),
                                      reads=[("pb", bank)],
                                      writes=[(dname, j, "c", qg)],
                                      tag=gq + dname + "ev")
                                p.add("vector", lambda e, dst=dst, pin=pin:
                                      e.scalar_tensor_tensor(
                                          dst, pin, SC_LO, dst, AL.mult,
                                          AL.add),
                                      reads=[("pb", bank),
                                             (dname, j, "c", qg)],
                                      writes=[(dname, j, "c", qg)],
                                      tag=gq + dname + "ev2")
            for hf in range(2):
                trd = b5_begin()
                for tl in range(8):
                    t = hf * 8 + tl
                    p.add("tensor", lambda e, t=t, tl=tl: e.transpose(
                        pb_bf.ap()[:, tl * 128:tl * 128 + 128],
                        ov["qrall_bf"][:, t * 128:(t + 1) * 128], ident[:]),
                        reads=[("qrall_bf", hf), ("ident",)] +
                        (trd if tl == 0 else []),
                        writes=[("pb", 5, tl)],
                        tag=gq + "qrtr")
                mk = b5_end(("vector", "scalar"))
                for tl in range(8):
                    t = hf * 8 + tl
                    for j in range(HG):
                        eng = "vector" if j < 2 else "scalar"
                        first = (tl == 0 and j in (0, 2))
                        last = (tl == 7 and j in (1, 3))

                        def _em(e, t=t, j=j, tl=tl):
                            f = e.tensor_copy if j < 2 else e.copy
                            return f(ov["Qst"][64:96, j * S + t * 128:
                                               j * S + (t + 1) * 128],
                                     pb_bf.ap()[j * 32:j * 32 + 32,
                                                tl * 128:tl * 128 + 128])
                        p.add(eng, _em,
                              reads=[("pb", 5, i) for i in range(8)]
                              if first else [("pb", 5, tl)],
                              writes=[("Qst", j, "r", t)] +
                              ([mk[eng]] if last else []),
                              tag=gq + "qrtrev")
            V3 = ov["V_g"].ap().rearrange("p (t h c) -> p t h c", h=HG, c=65)
            p.add("vector", lambda e, V3=V3: e.memset(V3[:, :, :, 64:65], 1.0),
                  writes=[("V_g", "ones")], tag=gq + "vones")
            for kt in range(NT):
                qb = 4 if kt % 2 == 0 else 6
                for lc in range(2):
                    p.add("tensor", lambda e, kt=kt, lc=lc, g=g, qb=qb:
                          e.matmul(
                        pb[qb][:, 0:256],
                        ov["c_kv"][:, lc * S + kt * 128:
                                   lc * S + (kt + 1) * 128],
                        ov["wuv"][:, lc * D + g * 256:lc * D + (g + 1) * 256],
                        start=(lc == 0), stop=(lc == 1)),
                        reads=[("c_kv", lc, kt // 4), ("wuv",)],
                        writes=[("pb", qb)], tag=gq + "vmm")
                p.add("vector", lambda e, kt=kt, V3=V3, qb=qb: e.tensor_copy(
                    V3[:, kt, :, 0:64],
                    pb[qb][:, 0:256].rearrange("p (h c) -> p h c", c=64)),
                    reads=[("pb", qb), ("V_g", "ones")], writes=[("V_g", kt)],
                    tag=gq + "vev")

            def mk_block(j, qc, sbanks, ebufs, abank, g=None, ebank=4):
                g_ = g if g is not None else 0
                q0 = qc * 512
                nkt = 4 * qc + 4
                pa = pb[abank]
                hh = g_ * HG + j

                def scores(kt):
                    r = kt - 4 * qc
                    coloff = 0 if r < 0 else 128 * r
                    sbank = sbanks[kt % 2]
                    eb = ebufs[kt % 2]
                    sb_ = pb[sbank]
                    sc = S_scr[eb]
                    p.add("tensor", lambda e, kt=kt, coloff=coloff, sb_=sb_:
                          e.matmul(sb_[:, coloff:512],
                                   ov["Kst"][0:96, j * S + kt * 128:
                                             j * S + (kt + 1) * 128],
                                   ov["Qst"][0:96, j * S + q0 + coloff:
                                             j * S + q0 + 512],
                                   start=True, stop=True),
                          reads=[("Kst", j, "c", kt // 4), ("Kst", j, "r"),
                                 ("Qst", j, "c", qc)] +
                          [("Qst", j, "r", qc * 4 + i) for i in range(4)],
                          writes=[("pb", sbank)], tag=f"s{qc}smm")
                    if r >= 0:
                        p.add("vector", lambda e, kt=kt, coloff=coloff,
                              sb_=sb_, sc=sc:
                              e.scalar_tensor_tensor(
                                  sc[:, coloff:512],
                                  sb_[:, coloff:512], -CLIP_HI,
                                  tri[:, 0:512 - coloff], AL.max, AL.min),
                              reads=[("pb", sbank), ("tri",)],
                              writes=[(f"S_scr{eb}", 0), (f"S_scr{eb}", 1)],
                              tag="clipd")
                    else:
                        p.add("vector", lambda e, kt=kt, sb_=sb_, sc=sc:
                              e.tensor_scalar(sc[:, 0:512], sb_[:, 0:512],
                                              CLIP_HI, -CLIP_HI,
                                              AL.min, AL.max),
                              reads=[("pb", sbank)],
                              writes=[(f"S_scr{eb}", 0), (f"S_scr{eb}", 1)],
                              tag="clip")
                def expf(kt):
                    r = kt - 4 * qc
                    coloff = 0 if r < 0 else 128 * r
                    eb = ebufs[kt % 2]
                    sc = S_scr[eb]
                    p.add("scalar", lambda e, kt=kt, coloff=coloff, sc=sc,
                          eb=eb:
                          e.activation(E_T[eb][:, coloff:512],
                                       sc[:, coloff:512], AF.Exp),
                          reads=[(f"S_scr{eb}", 0), (f"S_scr{eb}", 1)],
                          writes=[(f"E_T{eb}",)], tag="exp")

                def av(kt):
                    r = kt - 4 * qc
                    coloff = 0 if r < 0 else 128 * r
                    eb = ebufs[kt % 2]
                    p.add("tensor", lambda e, kt=kt, coloff=coloff, eb=eb:
                          e.matmul(pa[0:65, coloff:512],
                                   ov["V_g"][:, kt * HG * 65 + j * 65:
                                             kt * HG * 65 + j * 65 + 65],
                                   E_T[eb][:, coloff:512],
                                   start=(kt == 0), stop=(kt == nkt - 1)),
                          reads=[("V_g", kt), ("V_g", "ones"),
                                 (f"E_T{eb}",)],
                          writes=[("pb", abank)], tag="av")

                par = abank - 2
                rf, rb, at = rs_f[par], rs_bf[par], attn_tmp[par]

                def epi1():
                    p.add("scalar", lambda e: e.activation(
                        rf[0:1, :], pa[64:65, :], AF.Ln),
                        reads=[("pb", abank)], writes=[("rs_f", par)],
                        small=True, tag="recip")
                    p.add("scalar", lambda e: e.activation(
                        rb[0:1, :], rf[0:1, :], AF.Exp, scale=-1.0),
                        reads=[("rs_f", par)], writes=[("rs_bf", par)],
                        small=True, tag="recipc")
                    p.add("scalar", lambda e: e.copy(
                        at[:], pa[0:64, :]),
                        reads=[("pb", abank)], writes=[("attn_tmp", par)],
                        tag="aev")

                def epi2():
                    h = hh
                    bb = ebank
                    p.add("tensor", lambda e, bb=bb: e.matmul(
                        pb[bb][0:64, :], ones64[:], rb[0:1, :],
                        start=True, stop=True),
                        reads=[("rs_bf", par), ("ones64",)],
                        writes=[("pb", bb)], tag="bcast")
                    p.add("vector", lambda e, h=h, q0=q0, bb=bb, afb=afb:
                          e.tensor_tensor(
                              afb[(h % 2) * 64:(h % 2) * 64 + 64,
                                  (h // 2) * S + q0:
                                  (h // 2) * S + q0 + 512],
                              at[:], pb[bb][0:64, :], AL.mult),
                          reads=[("attn_tmp", par), ("pb", bb)],
                          writes=[(afn, h, qc)] +
                          [(afn, qc * 4 + i) for i in range(4)], tag="anorm")
                return scores, expf, av, (epi1, epi2), nkt

            for qc in range(4):
                for jp in (0, 2):
                    eb2 = 1 if (qc == 3 and jp == 2) else 4
                    sA, xA, aA, eA, nkt = mk_block(jp, qc, (0, 1), (0, 1), 2,
                                                   g=g, ebank=eb2)
                    sB, xB, aB, eB, _ = mk_block(jp + 1, qc, (6, 7), (2, 3),
                                                 3, g=g, ebank=eb2)
                    for kt in range(nkt):
                        sA(kt)
                        sB(kt)
                        if pend_epi[0] is not None:
                            if kt == 1:
                                pend_epi[0][0][0]()
                                pend_epi[0][1][0]()
                            elif kt == 3 or (nkt <= 3 and kt == nkt - 1):
                                pend_epi[0][0][1]()
                                pend_epi[0][1][1]()
                                pend_epi[0] = None
                        if kt >= 2:
                            aA(kt - 2)
                            aB(kt - 2)
                        xA(kt)
                        xB(kt)
                    aA(nkt - 2)
                    aB(nkt - 2)
                    aA(nkt - 1)
                    aB(nkt - 1)
                    if pend_epi[0] is not None:
                        for blk in pend_epi[0]:
                            blk[0]()
                            blk[1]()
                        pend_epi[0] = None
                    pend_epi[0] = (eA, eB)

        if pend_epi[0] is not None:
            for blk in pend_epi[0]:
                blk[0]()
                blk[1]()
            pend_epi[0] = None

        pend_cast = []

        def kv_load(e_idx):
            par = e_idx % 2
            kname = "keysb0" if par == 0 else "keysb1"
            vname = "valsb0" if par == 0 else "valsb1"
            if seq > 0:
                p.add("sync", lambda e, e_idx=e_idx, kname=kname: e.dma_start(
                    out=ov[kname][:, 0:4096],
                    in_=kvscr[e_idx][0:4].rearrange("c p n -> p c n")),
                    reads=[("kvscr", e_idx, pp, h) for pp in range(4)
                           for h in range(2)],
                    writes=[(kname, pp) for pp in range(4)],
                    queue="qsync", slot=kname, tag=sq + "kvd")
                p.add("sync", lambda e, e_idx=e_idx, vname=vname: e.dma_start(
                    out=ov[vname][:, 0:4096],
                    in_=kvscr[e_idx][4:8].rearrange("c p n -> p c n")),
                    reads=[("kvscr", e_idx, 4 + pp, h) for pp in range(4)
                           for h in range(2)],
                    writes=[(vname, pp) for pp in range(4)],
                    queue="qsync", slot=vname, tag=sq + "kvd")
                return kname, vname
            def mk(piece):
                land = ov["land0"]
                sl = piece % 2
                lkey = ("land0", sl)
                if piece < 8:
                    src3 = keys_d[e_idx].rearrange(
                        "(c p) f -> p c f", p=128)[:, piece:piece + 1, :]
                    dname, dcol = kname, piece * 512
                else:
                    pv = piece - 8
                    src3 = vals_d[e_idx].rearrange(
                        "(c p) n -> p c n", p=128)[
                        :, pv // 2:pv // 2 + 1,
                        (pv % 2) * 512:(pv % 2) * 512 + 512]
                    dname, dcol = vname, pv * 512
                ceng = "gpsimd"
                lc0 = sl * 512

                def _in(src3=src3, land=land, lkey=lkey, lc0=lc0, sl=sl):
                    p.add("sync", lambda e: e.dma_start(
                        out=land[:, lc0:lc0 + 512], in_=src3),
                        writes=[lkey], queue="qsync", slot=f"l0{sl}",
                        tag=sq + "kvd")

                def _cast(e2, land=land, dname=dname, dcol=dcol, lc0=lc0):
                    return e2.tensor_copy(ov[dname][:, dcol:dcol + 512],
                                          land[:, lc0:lc0 + 512])

                def _out(e2, e_idx=e_idx, piece=piece, dname=dname,
                         dcol=dcol):
                    op = piece // 2 if piece < 8 else 4 + (piece - 8) // 2
                    oc = (piece % 2 if piece < 8 else (piece - 8) % 2) * 512
                    return e2.dma_start(
                        out=kvscr[e_idx][op][:, oc:oc + 512],
                        in_=ov[dname][:, dcol:dcol + 512])
                dpp = piece // 2 if piece < 8 else (piece - 8) // 2
                return (_in, ceng, _cast, _out, lkey, dname, dpp,
                        e_idx, piece)

            pieces = [mk(piece) for piece in range(16)]
            pieces[0][0]()
            pieces[1][0]()
            for k in range(16):
                nxt = pieces[k + 2][0] if k + 2 < 16 else None
                pend_cast.append(pieces[k] + (nxt,))
            return kname, vname

        def drip_cast(n=1):
            for _ in range(min(n, len(pend_cast))):
                (_in, ceng, fn, outfn, lkey, dname, pp, e_idx, piece,
                 next_in) = pend_cast.pop(0)
                op = piece // 2 if piece < 8 else 4 + (piece - 8) // 2
                p.add(ceng, fn, reads=[lkey], writes=[(dname, pp)],
                      tag=sq + "kvc")
                p.add("gpsimd", outfn, reads=[(dname, pp)],
                      writes=[("kvscr", e_idx, op, piece % 2),
                              ("scro", piece % 2)],
                      queue="qgp", slot="scro" + str(piece % 2),
                      tag=sq + "kvso")
                if next_in is not None:
                    next_in()

        load_weights_dma(("wo",), sq + "wo")
        kv_names = {0: kv_load(0)}
        for t in range(NT):
            xr = xrot[t % 2]
            p.add("sync", lambda e, t=t, xr=xr, seq=seq: e.dma_start(
                out=xr[:], in_=x_d[seq][t * 128:(t + 1) * 128, :]),
                writes=[(f"xrot{t % 2}",)], queue="qsync", tag=sq + f"xl2_{t}")
            for half in range(2):
                for ch in range(8):
                    p.add("tensor", lambda e, t=t, half=half, ch=ch, afb=afb:
                          e.matmul(pb[half][:, :],
                                   afb[:, ch * S + t * 128:
                                       ch * S + (t + 1) * 128],
                                   ov["wo"][:, ch * D + half * 512:
                                            ch * D + half * 512 + 512],
                                   start=(ch == 0), stop=(ch == 7)),
                          reads=[("wo",), (afn, 2 * ch, t // 4),
                                 (afn, 2 * ch + 1, t // 4)],
                          writes=[("pb", half)], tag=sq + "omm")
                p.add("vector", lambda e, t=t, half=half, xr=xr:
                      e.tensor_tensor(
                          ov["xp"][:, t * D + half * 512:
                                   t * D + half * 512 + 512],
                          pb[half][:, :], xr[:, half * 512:half * 512 + 512],
                          AL.add),
                      reads=[("pb", half), (f"xrot{t % 2}",)],
                      writes=[("xp", t, half)], tag=sq + "oadd")
            if t >= 2:
                ln_stage2(t - 2, sq + f"D{t - 2}_", xtn, xtb, part="tr")
                ln_stage2(t - 2, sq + f"D{t - 2}_", xtn, xtb, part="ev")
            srcA = ov["xp"].ap()[:, t * D:t * D + 512]
            srcB = ov["xp"].ap()[:, t * D + 512:t * D + 1024]
            ln_stats(srcA, srcB, [("xp", t, 0)], [("xp", t, 1)],
                     t, sq + f"D{t}_")
            ln_norm(srcA, srcB, [("xp", t, 0)], [("xp", t, 1)],
                    t, sq + f"D{t}_")
            if t == NT - 1:
                ln_stage2(t - 1, sq + f"D{t - 1}_", xtn, xtb)
                ln_stage2(t, sq + f"D{t}_", xtn, xtb)
            def gates_tile(t):
                for xc in range(8):
                    p.add("tensor", lambda e, t=t, xc=xc, xtb=xtb: e.matmul(
                        pb[3][:, 0:32],
                        xtb[:, xc * S + t * 128:xc * S + (t + 1) * 128],
                        selT_bf[:, xc * 32:(xc + 1) * 32],
                        start=(xc == 0), stop=(xc == 7)),
                        reads=[(xtn, t), ("selT_bf",)], writes=[("pb", 3)],
                        tag=sq + "gmm")
                p.add("scalar", lambda e, t=t: e.activation(
                    g_work[:, t * NE:(t + 1) * NE], pb[3][:, 0:32], AF.Exp,
                    scale=-1.0),
                    reads=[("pb", 3)], writes=[("g_work", "sig", t)],
                    tag=sq + "gsig")
                p.add("vector", lambda e, t=t: e.tensor_scalar(
                    g_ind[:, t * NE:(t + 1) * NE],
                    g_work[:, t * NE:(t + 1) * NE], 1.0, None, AL.add),
                    reads=[("g_work", "sig", t)], writes=[("g_ind", "sig", t)],
                    small=True, tag=sq + "gsig2")
                p.add("vector", lambda e, t=t: e.reciprocal(
                    gates_all[:, t * NE:(t + 1) * NE],
                    g_ind[:, t * NE:(t + 1) * NE]),
                    reads=[("g_ind", "sig", t)], writes=[("gates_all", t)],
                    small=True, tag=sq + "gsig3")
            if t >= 2:
                gates_tile(t - 2)
            if t == NT - 1:
                gates_tile(t - 1)
                gates_tile(t)
            drip_cast(1)
        kv_names[1] = kv_load(1)

        g3 = g_work.ap().rearrange("p (t e) -> p t e", e=NE)
        ga3 = gates_all.ap().rearrange("p (t e) -> p t e", e=NE)
        gi3 = g_ind.ap().rearrange("p (t e) -> p t e", e=NE)
        gr = [("gates_all", t) for t in range(NT)]
        p.add("vector", lambda e: e.tensor_copy(g_work[:], gates_all[:]),
              reads=gr, writes=[("g_work",)], tag=sq + "t4cp")
        for rnd in range(4):
            m2 = g_m.ap()[:, rnd * NT:(rnd + 1) * NT].rearrange(
                "p (t o) -> p t o", o=1)
            p.add("vector", lambda e, m2=m2: e.tensor_reduce(
                m2, g3, AX, AL.max),
                reads=[("g_work",)], writes=[("g_m", rnd)], small=True,
                tag=sq + f"t4m{rnd}")
            if rnd < 3:
                mb = m2.broadcast_to((128, NT, NE))
                p.add("vector", lambda e, mb=mb: e.tensor_tensor(
                    gi3, g3, mb, AL.is_ge),
                    reads=[("g_work",), ("g_m", rnd)], writes=[("g_ind",)],
                    tag=sq + f"t4i{rnd}")
                p.add("vector", lambda e: e.scalar_tensor_tensor(
                    g3, gi3, -1e9, g3, AL.mult, AL.add),
                    reads=[("g_ind",), ("g_work",)], writes=[("g_work",)],
                    tag=sq + f"t4u{rnd}")
        m4 = g_m.ap()[:, 3 * NT:4 * NT].rearrange("p (t o) -> p t o", o=1)
        m4b = m4.broadcast_to((128, NT, NE))
        p.add("vector", lambda e: e.tensor_tensor(gi3, ga3, m4b, AL.is_ge),
              reads=gr + [("g_m", 3)], writes=[("g_ind",)], tag=sq + "t4fi")
        p.add("vector", lambda e: e.tensor_tensor(
            gate_full[:], gates_all[:], g_ind[:], AL.mult),
            reads=gr + [("g_ind",)],
            writes=[("gate_full", t) for t in range(NT)], tag=sq + "t4gf")

        if seq + 1 < nseq:
            wl = emit_A_chunks(seq + 1, woven=True)
            woven.add(seq + 1)
        else:
            wl = []
        drip_cast(16)
        for e_idx in range(NE):
            kname, vname = kv_names[e_idx]
            if e_idx + 1 < NE and e_idx + 1 not in kv_names:
                kv_names[e_idx + 1] = kv_load(e_idx + 1)
            kdst, vdst = ov[kname], ov[vname]

            def h_stage(grp, e_idx=e_idx, kname=kname, kdst=kdst):
                kreads = [(kname, pc) for pc in range(4)]
                for fc in range(4):
                    for xc in range(8):
                        p.add("tensor", lambda e, grp=grp, fc=fc, xc=xc, xtb=xtb:
                              e.matmul(pb[fc][:, :],
                                       kdst[:, xc * EF + fc * 128:
                                            xc * EF + fc * 128 + 128],
                                       xtb[:, xc * S + grp * 512:
                                           xc * S + (grp + 1) * 512],
                                       start=(xc == 0), stop=(xc == 7)),
                              reads=[(xtn, 4 * grp), (xtn, 4 * grp + 1),
                                     (xtn, 4 * grp + 2),
                                     (xtn, 4 * grp + 3)] + kreads,
                              writes=[("pb", fc)], tag=sq + "hmm")
                hname = "h_sb0" if grp % 2 == 0 else "h_sb1"
                hsb = ov[hname]
                for fc in range(4):
                    p.add("scalar", lambda e, fc=fc, hsb=hsb: e.activation(
                        hsb[:, fc * 512:(fc + 1) * 512], pb[fc][:, :],
                        AF.Relu),
                        reads=[("pb", fc)], writes=[(hname, fc)],
                        tag=sq + "hrelu")
                return hname, hsb

            def y_stage(grp, hname, hsb, vname=vname, vdst=vdst,
                        e_idx=e_idx):
                vreads = [(vname, pc) for pc in range(4)]
                hreads = [(hname, fc) for fc in range(4)]
                YB = (4, 6, 7)
                for tl in range(4):
                    t = grp * 4 + tl
                    for half in range(2):
                        yb = YB[(tl * 2 + half) % 3]
                        for fc in range(4):
                            p.add("tensor", lambda e, fc=fc, half=half,
                                  hsb=hsb, tl=tl, yb=yb:
                                  e.matmul(pb[yb][:, :],
                                           hsb[:, fc * 512 + tl * 128:
                                               fc * 512 + tl * 128 + 128],
                                           vdst[:, fc * D + half * 512:
                                                fc * D + half * 512 + 512],
                                           start=(fc == 0), stop=(fc == 3)),
                                  reads=hreads + vreads,
                                  writes=[("pb", yb)], tag=sq + "ymm")
                        p.add("vector", lambda e, t=t, half=half,
                              e_idx=e_idx, yb=yb:
                              e.scalar_tensor_tensor(
                                  ov["xp"][:, t * D + half * 512:
                                           t * D + half * 512 + 512],
                                  pb[yb][:, :],
                                  gate_full.ap()[:, t * NE + e_idx:
                                                 t * NE + e_idx + 1],
                                  ov["xp"][:, t * D + half * 512:
                                           t * D + half * 512 + 512],
                                  AL.mult, AL.add),
                              reads=[("pb", yb), ("xp", t, half),
                                     ("gate_full", t)],
                              writes=[("xp", t, half)], tag=sq + "yadd")

            prev = None
            for grp in range(4):
                drip_cast(8 if grp < 2 else 0)
                if wl:
                    wl.pop(0)()
                cur = (grp,) + h_stage(grp)
                if prev is not None:
                    y_stage(*prev)
                prev = cur
            y_stage(*prev)

        while wl:
            wl.pop(0)()

        for t in range(NT):
            p.add("scalar", lambda e, t=t, seq=seq: e.dma_start(
                out=y_d[seq][t * 128:(t + 1) * 128, :],
                in_=ov["xp"][:, t * D:(t + 1) * D]),
                reads=[("xp", t, 0), ("xp", t, 1)], queue="qsc",
                writes=[("yst", t % 4)],
                slot=f"st{t % 4}", must_inc=True, tag=sq + f"store{t}")

    p.finalize_and_emit(nc)
    return nc


def kernel(L0_attn_x, L0_attn_W_DQ, L0_attn_W_UQ, L0_attn_W_QR, L0_attn_W_DKV,
           L0_attn_W_UK, L0_attn_W_UV, L0_attn_W_KR, L0_attn_W_O,
           L0_ffn_expert_sel, L0_ffn_keys, L0_ffn_values):
    x = np.ascontiguousarray(np.asarray(L0_attn_x))
    common = {
        "wdq": np.ascontiguousarray(L0_attn_W_DQ),
        "wuq": np.ascontiguousarray(L0_attn_W_UQ),
        "wqr": _perm_qr(L0_attn_W_QR),
        "wdkv": np.ascontiguousarray(L0_attn_W_DKV),
        "wuk": np.ascontiguousarray(L0_attn_W_UK),
        "wuv": np.ascontiguousarray(L0_attn_W_UV),
        "wkr": _perm_qr(L0_attn_W_KR),
        "wo": np.ascontiguousarray(L0_attn_W_O),
        "selT": np.ascontiguousarray(np.asarray(L0_ffn_expert_sel).T),
        "keys": np.ascontiguousarray(L0_ffn_keys),
        "vals": np.ascontiguousarray(L0_ffn_values),
    }
    nc = build_program()
    in_maps = [dict(common, x=np.ascontiguousarray(x[c * BPC:(c + 1) * BPC]))
               for c in range(NCORES)]
    res = run_bass_kernel_spmd(nc, in_maps, core_ids=list(range(NCORES)))
    return np.concatenate([r["y"] for r in res.results], axis=0)


# version 22
# speedup vs baseline: 1.1050x; 1.1050x over previous
import math
import contextlib
import numpy as np
import ml_dtypes
import concourse.bass as bass
import concourse.mybir as mybir
from concourse.bass_utils import run_bass_kernel_spmd

F32 = mybir.dt.float32
BF16 = mybir.dt.bfloat16
AL = mybir.AluOpType
AF = mybir.ActivationFunctionType
AX = mybir.AxisListType.X

B, S, D = 16, 2048, 1024
NCORES = 8
BPC = B // NCORES
NH, DH, DHR, DL = 16, 64, 32, 256
NE, EF = 32, 512
HG = 4
NG = NH // HG
NT = S // 128
SCALE = 1.0 / float(np.float32(np.sqrt(np.float32(96.0))))
SC_HI = float(np.float32(SCALE))
SC_LO = float(np.float32(SCALE - SC_HI))
CLIP_HI = 80.0 * SCALE
KB = 1024

COMPUTE_ENGINES = ("tensor", "vector", "scalar", "gpsimd", "sync")
QUEUES = {"qsync": "sync", "qsc": "scalar", "qgp": "gpsimd"}
STREAMS = COMPUTE_ENGINES + tuple(QUEUES)


class Prog:

    def __init__(self):
        self.issue = {s: [] for s in STREAMS}
        self.cstreams = {}
        self.writer = {}
        self.readers = {}
        self.conflicts = {}
        self.name_keys = {}
        self.active_names = set()
        self.order = 0

    def set_conflicts(self, table):
        self.conflicts = table

    def add(self, eng, emit, reads=(), writes=(), queue=None, must_inc=False,
            small=False, slot=None, tag=""):
        issue_stream = queue if queue is not None else eng
        if queue is not None:
            must_inc = True
            if slot is None:
                b0 = writes[0] if writes else ("_",)
                slot = "_".join(str(x) for x in b0) if isinstance(b0, tuple) \
                    else str(b0)
            cstream = (queue, slot)
        else:
            cstream = eng
        cidx = self.cstreams.get(cstream, 0)
        self.cstreams[cstream] = cidx + 1
        waits = {}

        def dep(pcs, pci, raw=False):
            if pcs == cstream and not isinstance(cstream, tuple) \
                    and cstream not in ("vector", "scalar", "gpsimd"):
                return
            if queue is not None and pcs == eng and raw \
                    and eng != "gpsimd":
                return
            if waits.get(pcs, -1) < pci:
                waits[pcs] = pci

        for b in reads:
            w = self.writer.get(b)
            if w is not None:
                dep(w[0], w[1], raw=True)
        for b in writes:
            for (rs, ri) in self.readers.get(b, ()):
                dep(rs, ri)
            w = self.writer.get(b)
            if w is not None:
                dep(w[0], w[1])
        wnames = {b[0] if isinstance(b, tuple) else b for b in writes}
        for n in wnames:
            conf = self.conflicts.get(n)
            if not conf:
                continue
            for other in conf & self.active_names:
                for k in self.name_keys.get(other, ()):
                    w = self.writer.get(k)
                    if w is not None:
                        dep(w[0], w[1])
                    for (rs, ri) in self.readers.get(k, ()):
                        dep(rs, ri)

        op = dict(eng=eng, emit=emit, waits=waits,
                  queue=queue, cstream=cstream, cidx=cidx, inc=None,
                  must_inc=must_inc, order=self.order, tag=tag)
        self.order += 1
        self.issue[issue_stream].append(op)
        for b in writes:
            self.writer[b] = (cstream, cidx, small)
            self.readers[b] = []
            n = b[0] if isinstance(b, tuple) else b
            self.name_keys.setdefault(n, set()).add(b)
            self.active_names.add(n)
        for b in reads:
            self.readers.setdefault(b, []).append((cstream, cidx))
        return (cstream, cidx)

    def finalize_and_emit(self, nc):
        needed = set()
        allops = [op for s in STREAMS for op in self.issue[s]]
        for op in allops:
            for pcs, pci in op["waits"].items():
                needed.add((pcs, pci))
        inc_no = {}
        counts = {}
        for s in STREAMS:
            for op in self.issue[s]:
                cs, ci = op["cstream"], op["cidx"]
                if (cs, ci) in needed or op["must_inc"]:
                    op["_inc_pending"] = True
        by_cs = {}
        for op in allops:
            by_cs.setdefault(op["cstream"], []).append(op)
        totals = {}
        for cs, ops in by_cs.items():
            ops.sort(key=lambda o: o["cidx"])
            cnt = 0
            for op in ops:
                if op.get("_inc_pending"):
                    cnt += 1
                    inc_no[(cs, op["cidx"])] = cnt
                    op["inc"] = cnt
            totals[cs] = cnt

        def unit(cs):
            return 16 if isinstance(cs, tuple) else 1

        es = contextlib.ExitStack()
        with es:
            used = [cs for cs in by_cs if totals.get(cs, 0) > 0]
            sems = {cs: es.enter_context(nc.semaphore(
                ("sem_" + "_".join(cs)) if isinstance(cs, tuple)
                else "sem_" + cs)) for cs in used}
            block = es.enter_context(nc.Block())

            def make_closure(engine_name):
                def body(eng):
                    items = []
                    for s in STREAMS:
                        if (s == engine_name and s in COMPUTE_ENGINES) or \
                           (s in QUEUES and QUEUES[s] == engine_name):
                            for op in self.issue[s]:
                                items.append((op["order"], op))
                    items.sort(key=lambda t: t[0])
                    waited = {}
                    for _, op in items:
                        for pcs in sorted(op["waits"],
                                          key=lambda c: str(c)):
                            pci = op["waits"][pcs]
                            val = inc_no[(pcs, pci)] * unit(pcs)
                            if waited.get(pcs, -1) >= val:
                                continue
                            eng.wait_ge(sems[pcs], val)
                            waited[pcs] = val
                        inst = op["emit"](eng)
                        if op["inc"] is not None:
                            inst.then_inc(sems[op["cstream"]],
                                          unit(op["cstream"]))
                    if engine_name == "gpsimd":
                        for cs in used:
                            if isinstance(cs, tuple):
                                eng.wait_ge(sems[cs], totals[cs] * unit(cs))
                return body

            block.tensor(make_closure("tensor"))
            block.vector(make_closure("vector"))
            block.scalar(make_closure("scalar"))
            block.gpsimd(make_closure("gpsimd"))
            block.sync(make_closure("sync"))
        return nc


def _rope_tables():
    import jax
    import jax.numpy as jnp
    with jax.default_device(jax.devices("cpu")[0]):
        inv = 10000.0 ** (-jnp.arange(0, DHR, 2, dtype=jnp.float32) / DHR)
        ang = jnp.arange(S, dtype=jnp.float32)[:, None] * inv
        cos = np.asarray(jnp.cos(ang))
        sin = np.asarray(jnp.sin(ang))

    def tok_major(tab):
        return tab.reshape(NT, 128, 16).transpose(1, 0, 2).reshape(
            128, NT * 16).astype(np.float32)

    return np.ascontiguousarray(
        np.concatenate([tok_major(cos), tok_major(sin)], axis=1))


def _tri_tile():
    i = np.arange(128)[:, None]
    j = np.arange(512)[None, :]
    return np.ascontiguousarray(
        np.where((j >= i) | (j >= 128), CLIP_HI, -1e30).astype(np.float32))


def _perm_qr(w):
    cols = []
    nh = w.shape[1] // DHR
    for h in range(nh):
        base = h * DHR
        cols.extend(base + 2 * np.arange(16))
        cols.extend(base + 2 * np.arange(16) + 1)
    return np.ascontiguousarray(np.asarray(w)[:, np.array(cols)])


def _rope(p, src, dst, cos4, bufs, nh, rkeys, wkeys, tag, trange=None):
    t0, t1 = trange if trange is not None else (0, NT)
    nt = t1 - t0
    s4 = src.ap().rearrange("p (t h s d) -> p t h s d", h=nh, s=2, d=16)[
        :, t0:t1]
    d4 = dst.ap().rearrange("p (t h s d) -> p t h s d", h=nh, s=2, d=16)[
        :, t0:t1]
    x1, x2 = s4[:, :, :, 0, :], s4[:, :, :, 1, :]
    r1, r2 = d4[:, :, :, 0, :], d4[:, :, :, 1, :]
    n = nt * nh * 16
    rA, rB, rC = bufs
    kA, kB = ("ropeA",), ("ropeB",)

    def t3(buf):
        return buf.ap()[:, 0:n].rearrange("p (t h d) -> p t h d", h=nh, d=16)

    def ctab(i):
        a = cos4.ap()[:, i * 256 + t0 * 16:i * 256 + t1 * 16].rearrange(
            "p (t o d) -> p t o d", o=1, d=16)
        return a.broadcast_to((128, nt, nh, 16))

    ch, sh = ctab(0), ctab(1)
    ck = [("cos4",)]
    A, Bv = t3(rA), t3(rB)

    def tt(out, i0, i1, op, rd, wr, tg):
        p.add("vector", lambda e: e.tensor_tensor(out, i0, i1, op),
              reads=rd, writes=wr, tag=tag + tg)

    tt(A, x1, ch, AL.mult, rkeys + ck, [kA], "p1")
    tt(Bv, x2, sh, AL.mult, rkeys + ck, [kB], "p2")
    tt(r1, A, Bv, AL.subtract, [kA, kB], wkeys, "w1")
    tt(A, x1, sh, AL.mult, rkeys + ck, [kA], "p3")
    tt(Bv, x2, ch, AL.mult, rkeys + ck, [kB], "p4")
    tt(r2, A, Bv, AL.add, [kA, kB], wkeys, "w2")


_SEQ_LIMIT = None


def build_program():
    nc = bass.Bass("TRN2", target_bir_lowering=False)
    p = Prog()

    x_d = nc.dram_tensor("x", [BPC, S, D], F32, kind="ExternalInput").ap()
    wdq_d = nc.dram_tensor("wdq", [D, DL], F32, kind="ExternalInput").ap()
    wuq_d = nc.dram_tensor("wuq", [DL, D], F32, kind="ExternalInput").ap()
    wqr_d = nc.dram_tensor("wqr", [DL, NH * 32], F32, kind="ExternalInput").ap()
    wdkv_d = nc.dram_tensor("wdkv", [D, DL], F32, kind="ExternalInput").ap()
    wuk_d = nc.dram_tensor("wuk", [DL, D], F32, kind="ExternalInput").ap()
    wuv_d = nc.dram_tensor("wuv", [DL, D], F32, kind="ExternalInput").ap()
    wkr_d = nc.dram_tensor("wkr", [D, 32], F32, kind="ExternalInput").ap()
    wo_d = nc.dram_tensor("wo", [D, D], F32, kind="ExternalInput").ap()
    selT_d = nc.dram_tensor("selT", [D, NE], F32, kind="ExternalInput").ap()
    keys_d = nc.dram_tensor("keys", [NE, D, EF], F32, kind="ExternalInput").ap()
    vals_d = nc.dram_tensor("vals", [NE, EF, D], F32, kind="ExternalInput").ap()
    y_d = nc.dram_tensor("y", [BPC, S, D], F32, kind="ExternalOutput").ap()
    kvscr = nc.dram_tensor("kvscr", [NE, 8, 128, 1024], BF16,
                           kind="Internal").ap()
    wscr = nc.dram_tensor("wscr", [128, 19712], BF16, kind="Internal").ap()

    cos_c = nc.inline_tensor(_rope_tables(), name="cos4c")
    tri_c = nc.inline_tensor(_tri_tile(), name="tric")
    id_c = nc.inline_tensor(
        np.eye(128, dtype=np.float32).astype(ml_dtypes.bfloat16), name="identc")
    idf_c = nc.inline_tensor(np.eye(128, dtype=np.float32), name="identfc")

    def sb(name, shape, dtype):
        return nc.alloc_sbuf_tensor(name, shape, dtype)

    xt_fm = sb("xt_fm", [128, 8 * S], BF16)
    attn_fm = sb("attn_fm", [128, 8 * S], BF16)
    stats = [sb(f"stats{i}", [128, 16], F32) for i in range(2)]
    smt = [sb(f"smt{i}", [128, 8], F32) for i in range(2)]
    eps_t = sb("eps_t", [128, 1], F32)
    cos4 = sb("cos4", [128, 2 * NT * 16], F32)
    tri = sb("tri", [128, 512], F32)
    ident = sb("ident", [128, 128], BF16)
    identf = sb("identf", [128, 128], F32)
    ones64 = sb("ones64", [1, 64], BF16)
    ones128 = sb("ones128", [64, 128], BF16)
    selT_bf = sb("selT_bf", [128, 8 * NE], BF16)
    gates_all = sb("gates_all", [128, NT * NE], F32)
    g_work = sb("g_work", [128, NT * NE], F32)
    g_ind = sb("g_ind", [128, NT * NE], F32)
    g_m = sb("g_m", [128, 4 * NT], F32)
    gate_full = sb("gate_full", [128, NT * NE], F32)
    E_T = [sb(f"E_T{i}", [128, 512], BF16) for i in range(2)]
    S_scr = [sb(f"S_scr{i}", [128, 512], F32) for i in range(2)]
    rs_f = [sb(f"rs_f{i}", [1, 512], BF16) for i in range(2)]
    rs_bf = [sb(f"rs_bf{i}", [1, 512], BF16) for i in range(2)]
    attn_tmp = [sb(f"attn_tmp{i}", [64, 512], BF16) for i in range(2)]
    lnscr = sb("lnscr", [128, D], F32)
    castbf = sb("castbf", [128, D], BF16)

    OV = 109 * KB
    arena = nc.alloc_sbuf_tensor("arena", [128, OV], mybir.dt.uint8)
    base = nc.lookup_mloc(arena).addr

    layout = {
        "xp":      (0 * KB, [128, NT * D], F32),
        "land0":   (64 * KB, [128, D], F32),
        "keysb0":  (68 * KB, [128, 8 * EF], BF16),
        "valsb0":  (76 * KB, [128, 4 * D], BF16),
        "keysb1":  (84 * KB, [128, 8 * EF], BF16),
        "valsb1":  (92 * KB, [128, 4 * D], BF16),
        "h_sb0":   (100 * KB, [128, 4 * EF], BF16),
        "h_sb1":   (104 * KB, [128, 4 * EF], BF16),
        "c_q":     (0 * KB, [128, 2 * S], BF16),
        "c_kv":    (8 * KB, [128, 2 * S], BF16),
        "wdq":     (16 * KB, [128, 8 * DL], BF16),
        "wdkv":    (20 * KB, [128, 8 * DL], BF16),
        "wuq":     (24 * KB, [128, 2 * D], BF16),
        "wuk":     (28 * KB, [128, 2 * D], BF16),
        "wuv":     (32 * KB, [128, 2 * D], BF16),
        "wqr":     (36 * KB, [128, 2 * 512], BF16),
        "wkr":     (38 * KB, [128, 8 * 32], BF16),
        "kr_all":  (38 * KB + 512, [128, NT * 32], F32),
        "krall_bf": (40 * KB + 512, [128, NT * 32], BF16),
        "Krst":    (41 * KB + 512, [32, S], BF16),
        "Qst":     (45 * KB + 512, [96, HG * S], BF16),
        "Kst":     (61 * KB + 512, [96, HG * S], BF16),
        "V_g":     (77 * KB + 512, [128, NT * HG * 65], BF16),
        "qr_all":  (86 * KB, [128, NT * 128], F32),
        "qrall_bf": (94 * KB, [128, NT * 128], BF16),
        "S_scr3":  (94 * KB, [128, 512], F32),
        "E_T3":    (96 * KB, [128, 512], BF16),
        "E_T2":    (98 * KB, [128, 512], BF16),
        "S_scr2":  (99 * KB, [128, 512], F32),
        "ropeA":   (101 * KB, [128, 1024], F32),
        "ropeB":   (105 * KB, [128, 1024], F32),
        "ropeC":   (77 * KB + 512, [128, 1024], F32),
        "cvl0":    (0 * KB, [128, D], F32),
        "cvl1":    (4 * KB, [128, D], F32),
        "cvl2":    (8 * KB, [128, D], F32),
        "cvl3":    (12 * KB, [128, D], F32),
        "cvb0":    (16 * KB, [128, D], BF16),
        "cvb1":    (18 * KB, [128, D], BF16),
        "wo":      (84 * KB, [128, 8 * D], BF16),
        "xrot0":   (100 * KB, [128, D], F32),
        "xrot1":   (104 * KB, [128, D], F32),
    }
    ov = {}
    iv = {}
    for nm, (off, shape, dt) in layout.items():
        nbytes = shape[1] * (2 if dt == BF16 else (1 if dt == mybir.dt.uint8 else 4))
        iv[nm] = (off, off + nbytes)
        assert off + nbytes <= OV, (nm, off, nbytes)
        ov[nm] = nc.alloc_sbuf_tensor_at(nm, shape, dt, offset=base + off)
    conf = {nm: set() for nm in layout}
    names = list(layout)
    for a in names:
        for b in names:
            if a >= b:
                continue
            (a0, a1), (b0, b1) = iv[a], iv[b]
            if a0 < b1 and b0 < a1:
                conf[a].add(b)
                conf[b].add(a)
    p.set_conflicts(conf)

    E_T.append(ov["E_T2"])
    E_T.append(ov["E_T3"])
    S_scr.append(ov["S_scr2"])
    S_scr.append(ov["S_scr3"])
    xrot = [ov["xrot0"], ov["xrot1"]]

    pb = []
    pb_bf = None
    for i in range(8):
        if i == 5:
            pb_bf = nc.alloc_psum_tensor("pbbf", [128, 1024], BF16)
            pb.append(pb_bf)
        else:
            pb.append(nc.alloc_psum_tensor(f"pb{i}", [128, 512], F32))

    b5s = {"t": 0, "engs": ()}

    def b5_begin():
        return [("b5m", b5s["t"], e) for e in b5s["engs"]]

    def b5_end(engs):
        b5s["t"] += 1
        b5s["engs"] = tuple(engs)
        return {e: ("b5m", b5s["t"], e) for e in engs}

    p.add("sync", lambda e: e.dma_start(out=cos4[:], in_=cos_c.ap()[:, :]),
          writes=[("cos4",)], queue="qsync")
    p.add("sync", lambda e: e.dma_start(out=tri[:], in_=tri_c.ap()[:, :]),
          writes=[("tri",)], queue="qsync")
    p.add("sync", lambda e: e.dma_start(out=ident[:], in_=id_c.ap()[:, :]),
          writes=[("ident",)], queue="qsync")
    p.add("sync", lambda e: e.dma_start(out=identf[:], in_=idf_c.ap()[:, :]),
          writes=[("identf",)], queue="qsync")
    p.add("vector", lambda e: e.memset(ones64[:], 1.0), writes=[("ones64",)])
    p.add("vector", lambda e: e.memset(ones128[:], 1.0), writes=[("ones128",)])
    p.add("vector", lambda e: e.memset(eps_t[:], 1e-5), writes=[("eps_t",)])
    selT_r = selT_d.rearrange("(c p) e -> p c e", p=128)
    p.add("sync", lambda e: e.dma_start(out=ov["land0"][:, 0:256], in_=selT_r),
          writes=[("land0",)], queue="qsync")
    p.add("vector", lambda e: e.tensor_copy(selT_bf[:], ov["land0"][:, 0:256]),
          reads=[("land0",)], writes=[("selT_bf",)])

    wsrc = [
        ("wdq", wdq_d.rearrange("(c p) l -> p c l", p=128), 8, DL),
        ("wdkv", wdkv_d.rearrange("(c p) l -> p c l", p=128), 8, DL),
        ("wuq", wuq_d.rearrange("(c p) n -> p c n", p=128), 2, D),
        ("wuk", wuk_d.rearrange("(c p) n -> p c n", p=128), 2, D),
        ("wuv", wuv_d.rearrange("(c p) n -> p c n", p=128), 2, D),
        ("wqr", wqr_d.rearrange("(c p) n -> p c n", p=128), 2, 512),
        ("wkr", wkr_d.rearrange("(c p) n -> p c n", p=128), 8, 32),
        ("wo", wo_d.rearrange("(c p) n -> p c n", p=128), 8, D),
    ]
    wbase = {}
    wnpieces = {}
    conv_pieces = []
    col0 = 0
    pcnt = [0]
    for wname, src3, nch, n in wsrc:
        wbase[wname] = col0
        per = max(1, 1024 // n)
        np_ = 0
        for i0 in range(0, nch, per):
            k = min(per, nch - i0)
            cols = k * n
            dst0 = col0 + i0 * n
            i = pcnt[0]
            pcnt[0] += 1
            lname = f"cvl{i % 4}"
            bname = f"cvb{i % 2}"
            ceng = "vector" if i % 2 == 0 else "scalar"

            def emit_piece(src3=src3, i0=i0, k=k, cols=cols, dst0=dst0,
                           lname=lname, bname=bname, ceng=ceng,
                           wname=wname, np_=np_):
                p.add("sync", lambda e: e.dma_start(
                    out=ov[lname][:, 0:cols], in_=src3[:, i0:i0 + k, :]),
                    writes=[(lname,)], queue="qsync", slot=lname,
                    tag="wcin_" + wname)

                def _cast(e):
                    f = e.tensor_copy if ceng == "vector" else e.copy
                    return f(ov[bname][:, 0:cols], ov[lname][:, 0:cols])
                p.add(ceng, _cast, reads=[(lname,)], writes=[(bname,)],
                      tag="wcc_" + wname)
                p.add("gpsimd", lambda e: e.dma_start(
                    out=wscr[:, dst0:dst0 + cols], in_=ov[bname][:, 0:cols]),
                    reads=[(bname,)], writes=[("wscr", wname, np_)],
                    queue="qgp", slot=bname, tag="wcout_" + wname)
            conv_pieces.append(emit_piece)
            np_ += 1
        wnpieces[wname] = np_
        col0 += nch * n
    assert col0 == 19712

    def load_weights_dma(names, tagname):
        for wname in names:
            cols = sum(nch * n for (w, _, nch, n) in wsrc if w == wname)
            p.add("sync", lambda e, wname=wname, cols=cols: e.dma_start(
                out=ov[wname][:, 0:cols],
                in_=wscr[:, wbase[wname]:wbase[wname] + cols]),
                reads=[("wscr", wname, i) for i in range(wnpieces[wname])],
                writes=[(wname,)], queue="qsync", slot="wld_" + wname,
                tag="wload_" + tagname)

    def _lnbufs(par):
        if par == 0:
            return ((S_scr[0], 0, [("S_scr0", 0), ("S_scr0", 1)]),
                    (S_scr[1], 0, [("S_scr1", 0), ("S_scr1", 1)]))
        if par == 1:
            return ((lnscr, 0, [("lnscr", 0)]),
                    (lnscr, 512, [("lnscr", 1)]))
        return ((ov["lnD2"], 0, [("lnD2", 0)]),
                (ov["lnD2"], 512, [("lnD2", 1)]))

    def _castbufs(par):
        if par == 0:
            return ((E_T[0], 0, ("E_T0",)),
                    (E_T[1], 0, ("E_T1",)))
        if par == 1:
            return ((castbf, 0, ("castbf", 0)),
                    (castbf, 512, ("castbf", 1)))
        return ((ov["castD2"], 0, ("castD2", 0)),
                (ov["castD2"], 512, ("castD2", 1)))

    def ln_stats(srcA, srcB, skA, skB, t, tag, par=None):
        par = t % 2 if par is None else par
        st = stats[par]
        p.add("vector", lambda e: e.bn_stats(st[:, 0:6], srcA),
              reads=skA, writes=[("stats", par, 0)], small=True,
              tag=tag + "bn0")
        p.add("vector", lambda e: e.bn_stats(st[:, 6:12], srcB),
              reads=skB, writes=[("stats", par, 1)], small=True,
              tag=tag + "bn1")
        p.add("vector", lambda e: e.bn_aggr(st[:, 12:14], st[:, 0:12]),
              reads=[("stats", par, 0), ("stats", par, 1)],
              writes=[("stats", par, 2)], small=True, tag=tag + "aggr")

    def ln_norm(srcA, srcB, skA, skB, t, tag, par=None):
        par = t % 2 if par is None else par
        st = stats[par]
        sm = smt[par]
        lb = _lnbufs(par)
        p.add("scalar", lambda e: e.activation(sm[:, 0:1], st[:, 13:14],
                                               AF.Ln, bias=eps_t.ap()),
              reads=[("stats", par, 2), ("eps_t",)], writes=[("smt", par, 0)],
              small=True, tag=tag + "sd")
        p.add("scalar", lambda e: e.activation(sm[:, 1:2], sm[:, 0:1],
                                               AF.Exp, scale=-0.5),
              reads=[("smt", par, 0)], writes=[("smt", par, 1)], small=True,
              tag=tag + "rstd")
        p.add("vector", lambda e: e.scalar_tensor_tensor(
            sm[:, 2:3], st[:, 12:13], -1.0, sm[:, 1:2], AL.mult, AL.mult),
            reads=[("stats", par, 2), ("smt", par, 1)],
            writes=[("smt", par, 2)], small=True, tag=tag + "nmr")
        for hf, (src, sk) in enumerate(((srcA, skA), (srcB, skB))):
            lt, lo, lk = lb[hf]
            p.add("scalar", lambda e, src=src, lt=lt, lo=lo: e.activation(
                lt.ap()[:, lo:lo + 512], src, AF.Identity,
                bias=sm.ap()[:, 2:3], scale=sm.ap()[:, 1:2]),
                reads=list(sk) + [("smt", par, 1), ("smt", par, 2)],
                writes=lk, tag=tag + f"ln{hf}")

    def ln_stage2(t, tag, xtn, xtb, part=None, par=None):
        par = t % 2 if par is None else par
        lb = _lnbufs(par)
        cb = _castbufs(par)
        if part in (None, "tr"):
            for hf in range(2):
                lt, lo, lk = lb[hf]
                ct, co, ck = cb[hf]
                p.add("scalar", lambda e, lt=lt, lo=lo, ct=ct, co=co: e.copy(
                    ct.ap()[:, co:co + 512], lt.ap()[:, lo:lo + 512]),
                    reads=lk, writes=[ck], tag=tag + f"cast{hf}")
            trd = b5_begin()
            for i in range(8):
                ct, co, ck = cb[i // 4]
                c0 = co + (i % 4) * 128
                p.add("tensor", lambda e, i=i, ct=ct, c0=c0: e.transpose(
                    pb_bf.ap()[:, i * 128:(i + 1) * 128],
                    ct.ap()[:, c0:c0 + 128], ident[:]),
                    reads=[ck, ("ident",)] + (trd if i == 0 else []),
                    writes=[("pb", 5, i)], tag=tag + f"tr{i}")
            if part == "tr":
                return
        mk = b5_end(("vector", "scalar"))
        xt3 = xtb.ap().rearrange("p (c n) -> p c n", n=S)
        pb3 = pb_bf.ap().rearrange("p (c n) -> p c n", n=128)
        p.add("vector", lambda e: e.tensor_copy(
            xt3[:, 0:4, t * 128:(t + 1) * 128], pb3[:, 0:4, :]),
            reads=[("pb", 5, i) for i in range(8)],
            writes=[(xtn, t, "lo"), mk["vector"]] +
            [(xtn, h, t // 4) for h in range(8)],
            tag=tag + "evlo")
        p.add("scalar", lambda e: e.copy(
            xt3[:, 4:8, t * 128:(t + 1) * 128], pb3[:, 4:8, :]),
            reads=[("pb", 5, i) for i in range(8)] + [(xtn, t, "lo")],
            writes=[(xtn, t), mk["scalar"]] +
            [(xtn, h, t // 4) for h in range(8, NH)],
            tag=tag + "evhi")

    def xtna(seq):
        return ("xt_fm", xt_fm) if seq % 2 == 0 else ("attn_fm", attn_fm)

    def afna(seq):
        return ("attn_fm", attn_fm) if seq % 2 == 0 else ("xt_fm", xt_fm)

    def emit_A_chunks(seq, woven):
        sq = f"s{seq}_"
        xtn, xtb = xtna(seq)
        per_t = []
        for t in range(NT):
            par = t % 2
            if not woven:
                xr = xrot[par]
                xk = (f"xrot{par}",)

                def c1(t=t, xr=xr, xk=xk):
                    p.add("sync", lambda e: e.dma_start(
                        out=xr[:], in_=x_d[seq][t * 128:(t + 1) * 128, :]),
                        writes=[xk], queue="qsync", slot=xk[0],
                        tag=sq + f"xload{t}")
                srcA = xr.ap()[:, 0:512]
                srcB = xr.ap()[:, 512:1024]
                skA = skB = [xk]
            else:
                def c1(t=t):
                    p.add("scalar", lambda e: e.dma_start(
                        out=g_work[:, 0:512],
                        in_=x_d[seq][t * 128:(t + 1) * 128, 0:512]),
                        writes=[("g_work",)], queue="qsc", slot="a1xa",
                        tag=sq + f"xload{t}a")
                    p.add("scalar", lambda e: e.dma_start(
                        out=g_ind[:, 0:512],
                        in_=x_d[seq][t * 128:(t + 1) * 128, 512:1024]),
                        writes=[("g_ind",)], queue="qsc", slot="a1xb",
                        tag=sq + f"xload{t}b")
                srcA = g_work.ap()[:, 0:512]
                srcB = g_ind.ap()[:, 0:512]
                skA = [("g_work",)]
                skB = [("g_ind",)]

            pr = t % 2

            def c2(t=t, srcA=srcA, srcB=srcB, skA=skA, skB=skB, pr=pr):
                ln_stats(srcA, srcB, skA, skB, t, sq + f"A{t}_", par=pr)

            def c3(t=t, srcA=srcA, srcB=srcB, skA=skA, skB=skB, pr=pr):
                ln_norm(srcA, srcB, skA, skB, t, sq + f"A{t}_", par=pr)

            def c4a(t=t, pr=pr):
                ln_stage2(t, sq + f"A{t}_", xtn, xtb,
                          part="tr" if woven else None, par=pr)

            def c4b(t=t, pr=pr):
                if woven:
                    ln_stage2(t, sq + f"A{t}_", xtn, xtb, part="ev", par=pr)
            per_t.append((c1, c2, c3, c4a, c4b))
        order = [per_t[0][0], per_t[0][1], per_t[0][2]]
        for t in range(1, NT):
            order += [per_t[t][0], per_t[t - 1][3], per_t[t][1],
                      per_t[t - 1][4], per_t[t][2]]
        order += [per_t[NT - 1][3], per_t[NT - 1][4]]
        return order

    nseq = BPC if _SEQ_LIMIT is None else _SEQ_LIMIT
    woven = set()
    for seq in range(nseq):
        sq = f"s{seq}_"
        xtn, xtb = xtna(seq)
        afn, afb = afna(seq)
        if seq not in woven:
            if seq == 0:
                for _ in range(4):
                    conv_pieces.pop(0)()
            for ch in emit_A_chunks(seq, woven=False):
                ch()
                if conv_pieces:
                    conv_pieces.pop(0)()
            while conv_pieces:
                conv_pieces.pop(0)()

        load_weights_dma(("wdq", "wdkv", "wuq", "wuk", "wuv", "wqr", "wkr"),
                         sq + "aw")

        for (wname, cname) in (("wdq", "c_q"), ("wdkv", "c_kv")):
            for lc in range(2):
                for qg in range(4):
                    bank = qg % 2
                    xreads = [(xtn, qg * 4 + i) for i in range(4)]
                    for xc in range(8):
                        p.add("tensor",
                              lambda e, lc=lc, qg=qg, xc=xc, wname=wname,
                              bank=bank, xtb=xtb:
                              e.matmul(pb[bank][:, :],
                                       ov[wname][:, xc * DL + lc * 128:
                                                 xc * DL + lc * 128 + 128],
                                       xtb[:, xc * S + qg * 512:
                                           xc * S + (qg + 1) * 512],
                                       start=(xc == 0), stop=(xc == 7)),
                              reads=[(wname,)] + xreads,
                              writes=[("pb", bank)], tag=sq + cname + "mm")
                    p.add("vector", lambda e, lc=lc, qg=qg, cname=cname,
                          bank=bank:
                          e.tensor_copy(ov[cname][:, lc * S + qg * 512:
                                                  lc * S + (qg + 1) * 512],
                                        pb[bank][:, :]),
                          reads=[("pb", bank)], writes=[(cname, lc, qg)],
                          tag=sq + cname + "ev")
        for t in range(NT):
            qb = 4 if t % 2 == 0 else 6
            for xc in range(8):
                p.add("tensor", lambda e, t=t, xc=xc, qb=qb, xtb=xtb: e.matmul(
                    pb[qb][:, 0:32],
                    xtb[:, xc * S + t * 128:xc * S + (t + 1) * 128],
                    ov["wkr"][:, xc * 32:(xc + 1) * 32],
                    start=(xc == 0), stop=(xc == 7)),
                    reads=[(xtn, t), ("wkr",)], writes=[("pb", qb)],
                    tag=sq + "krmm")
            p.add("vector", lambda e, t=t, qb=qb: e.tensor_copy(
                ov["kr_all"][:, t * 32:(t + 1) * 32], pb[qb][:, 0:32]),
                reads=[("pb", qb)], writes=[("kr_all", t)], tag=sq + "krev")
        for hf in range(2):
            _rope(p, ov["kr_all"], ov["krall_bf"], cos4,
                  (ov["ropeA"], ov["ropeB"], ov["ropeC"]), 1,
                  [("kr_all", t) for t in range(hf * 8, hf * 8 + 8)],
                  [("krall_bf", hf)], sq + f"krr{hf}",
                  trange=(hf * 8, hf * 8 + 8))
            trd = b5_begin()
            for tl in range(8):
                t = hf * 8 + tl
                p.add("tensor", lambda e, t=t, tl=tl: e.transpose(
                    pb_bf.ap()[0:32, tl * 128:tl * 128 + 128],
                    ov["krall_bf"][:, t * 32:(t + 1) * 32], ident[:]),
                    reads=[("krall_bf", hf), ("ident",)] +
                    (trd if tl == 0 else []),
                    writes=[("pb", 5, tl)],
                    tag=sq + "krtr")
            mkv = b5_end(("vector",))
            for tl in range(8):
                t = hf * 8 + tl
                p.add("vector", lambda e, t=t, tl=tl: e.tensor_scalar(
                    ov["Krst"][0:32, t * 128:(t + 1) * 128],
                    pb_bf.ap()[0:32, tl * 128:tl * 128 + 128],
                    SC_HI, None, AL.mult),
                    reads=[("pb", 5, i) for i in range(8)] if tl == 0
                    else [("pb", 5, tl)],
                    writes=[("Krst", t)], tag=sq + "krtrev")
                p.add("vector", lambda e, t=t, tl=tl: e.scalar_tensor_tensor(
                    ov["Krst"][0:32, t * 128:(t + 1) * 128],
                    pb_bf.ap()[0:32, tl * 128:tl * 128 + 128], SC_LO,
                    ov["Krst"][0:32, t * 128:(t + 1) * 128], AL.mult, AL.add),
                    reads=[("pb", 5, tl), ("Krst", t)],
                    writes=[("Krst", t)] + ([mkv["vector"]] if tl == 7
                                            else []),
                    tag=sq + "krtrev2")

        V3 = ov["V_g"].ap().rearrange("p (t h c) -> p t h c", h=HG, c=65)
        p.add("vector", lambda e: e.memset(V3[:, :, :, 64:65], 1.0),
              writes=[("V_g", "ones")], tag=sq + "vones")

        pend_epi = [None]
        pend_tail = [None]
        for g in range(NG):
            gq = f"{sq}g{g}_"
            if pend_tail[0] is not None:
                for f in pend_tail[0]:
                    f()
                pend_tail[0] = None
            if pend_epi[0] is not None:
                for blk in pend_epi[0]:
                    blk[0]()
                    blk[1]()
                pend_epi[0] = None
            for t in range(NT):
                qb = 4 if t % 2 == 0 else 6
                for lc in range(2):
                    p.add("tensor", lambda e, t=t, lc=lc, g=g, qb=qb: e.matmul(
                        pb[qb][:, 0:128],
                        ov["c_q"][:, lc * S + t * 128:lc * S + (t + 1) * 128],
                        ov["wqr"][:, lc * 512 + g * 128:
                                  lc * 512 + (g + 1) * 128],
                        start=(lc == 0), stop=(lc == 1)),
                        reads=[("c_q", lc, t // 4), ("wqr",)],
                        writes=[("pb", qb)], tag=gq + "qrmm")
                p.add("vector", lambda e, t=t, qb=qb: e.tensor_copy(
                    ov["qr_all"][:, t * 128:(t + 1) * 128], pb[qb][:, 0:128]),
                    reads=[("pb", qb)], writes=[("qr_all", t)], tag=gq + "qrev")
            for hf in range(2):
                _rope(p, ov["qr_all"], ov["qrall_bf"], cos4,
                      (ov["ropeA"], ov["ropeB"], ov["ropeC"]), HG,
                      [("qr_all", t) for t in range(hf * 8, hf * 8 + 8)],
                      [("qrall_bf", hf)], gq + f"qrr{hf}",
                      trange=(hf * 8, hf * 8 + 8))
            for j in range(HG):
                p.add("scalar", lambda e, j=j, g=g: e.dma_start(
                    out=ov["Kst"][64:96, j * S:(j + 1) * S],
                    in_=ov["Krst"][0:32, :]),
                    reads=[("Krst", t) for t in range(NT)],
                    writes=[("Kst", j, "r")], queue="qsc",
                    slot=f"krrep{j}", tag=gq + "krrep")
            for (wname, dname, cn) in (("wuq", "Qst", "c_q"),
                                       ("wuk", "Kst", "c_kv")):
                for ch2 in range(2):
                    for qg in range(4):
                        bank = qg % 2
                        for lc in range(2):
                            p.add("tensor",
                                  lambda e, wname=wname, ch2=ch2, qg=qg,
                                  lc=lc, bank=bank, g=g, cn=cn:
                                  e.matmul(pb[bank][:, :],
                                           ov[wname][:, lc * D + g * 256 +
                                                     ch2 * 128:
                                                     lc * D + g * 256 +
                                                     ch2 * 128 + 128],
                                           ov[cn][:, lc * S + qg * 512:
                                                  lc * S + (qg + 1) * 512],
                                           start=(lc == 0), stop=(lc == 1)),
                                  reads=[(wname,), (cn, lc, qg)],
                                  writes=[("pb", bank)], tag=gq + dname + "mm")
                        for jh in range(2):
                            j = 2 * ch2 + jh
                            dst = ov[dname][0:64, j * S + qg * 512:
                                            j * S + (qg + 1) * 512]
                            pin = pb[bank][jh * 64:jh * 64 + 64, :]
                            if dname == "Qst":
                                def _ev(e, dst=dst, pin=pin, jh=jh):
                                    f = e.tensor_copy if jh == 0 else e.copy
                                    return f(dst, pin)
                                p.add("vector" if jh == 0 else "scalar", _ev,
                                      reads=[("pb", bank)],
                                      writes=[(dname, j, "c", qg)],
                                      tag=gq + dname + "ev")
                            else:
                                p.add("vector", lambda e, dst=dst, pin=pin:
                                      e.tensor_scalar(dst, pin, SC_HI, None,
                                                      AL.mult),
                                      reads=[("pb", bank)],
                                      writes=[(dname, j, "c", qg)],
                                      tag=gq + dname + "ev")
                                p.add("vector", lambda e, dst=dst, pin=pin:
                                      e.scalar_tensor_tensor(
                                          dst, pin, SC_LO, dst, AL.mult,
                                          AL.add),
                                      reads=[("pb", bank),
                                             (dname, j, "c", qg)],
                                      writes=[(dname, j, "c", qg)],
                                      tag=gq + dname + "ev2")
            for hf in range(2):
                trd = b5_begin()
                for tl in range(8):
                    t = hf * 8 + tl
                    p.add("tensor", lambda e, t=t, tl=tl: e.transpose(
                        pb_bf.ap()[:, tl * 128:tl * 128 + 128],
                        ov["qrall_bf"][:, t * 128:(t + 1) * 128], ident[:]),
                        reads=[("qrall_bf", hf), ("ident",)] +
                        (trd if tl == 0 else []),
                        writes=[("pb", 5, tl)],
                        tag=gq + "qrtr")
                mk = b5_end(("vector", "scalar"))
                for tl in range(8):
                    t = hf * 8 + tl
                    for j in range(HG):
                        eng = "vector" if j < 2 else "scalar"
                        first = (tl == 0 and j in (0, 2))
                        last = (tl == 7 and j in (1, 3))

                        def _em(e, t=t, j=j, tl=tl):
                            f = e.tensor_copy if j < 2 else e.copy
                            return f(ov["Qst"][64:96, j * S + t * 128:
                                               j * S + (t + 1) * 128],
                                     pb_bf.ap()[j * 32:j * 32 + 32,
                                                tl * 128:tl * 128 + 128])
                        p.add(eng, _em,
                              reads=[("pb", 5, i) for i in range(8)]
                              if first else [("pb", 5, tl)],
                              writes=[("Qst", j, "r", t)] +
                              ([mk[eng]] if last else []),
                              tag=gq + "qrtrev")
            V3 = ov["V_g"].ap().rearrange("p (t h c) -> p t h c", h=HG, c=65)
            p.add("vector", lambda e, V3=V3: e.memset(V3[:, :, :, 64:65], 1.0),
                  writes=[("V_g", "ones")], tag=gq + "vones")
            for kt in range(NT):
                qb = 4 if kt % 2 == 0 else 6
                for lc in range(2):
                    p.add("tensor", lambda e, kt=kt, lc=lc, g=g, qb=qb:
                          e.matmul(
                        pb[qb][:, 0:256],
                        ov["c_kv"][:, lc * S + kt * 128:
                                   lc * S + (kt + 1) * 128],
                        ov["wuv"][:, lc * D + g * 256:lc * D + (g + 1) * 256],
                        start=(lc == 0), stop=(lc == 1)),
                        reads=[("c_kv", lc, kt // 4), ("wuv",)],
                        writes=[("pb", qb)], tag=gq + "vmm")
                p.add("vector", lambda e, kt=kt, V3=V3, qb=qb: e.tensor_copy(
                    V3[:, kt, :, 0:64],
                    pb[qb][:, 0:256].rearrange("p (h c) -> p h c", c=64)),
                    reads=[("pb", qb), ("V_g", "ones")], writes=[("V_g", kt)],
                    tag=gq + "vev")

            def mk_block(j, qc, sbanks, ebufs, abank, g=None, ebank=4):
                g_ = g if g is not None else 0
                q0 = qc * 512
                nkt = 4 * qc + 4
                pa = pb[abank]
                hh = g_ * HG + j

                def scores(kt):
                    r = kt - 4 * qc
                    coloff = 0 if r < 0 else 128 * r
                    sbank = sbanks[kt % 2]
                    eb = ebufs[kt % 2]
                    sb_ = pb[sbank]
                    sc = S_scr[eb]
                    p.add("tensor", lambda e, kt=kt, coloff=coloff, sb_=sb_:
                          e.matmul(sb_[:, coloff:512],
                                   ov["Kst"][0:96, j * S + kt * 128:
                                             j * S + (kt + 1) * 128],
                                   ov["Qst"][0:96, j * S + q0 + coloff:
                                             j * S + q0 + 512],
                                   start=True, stop=True),
                          reads=[("Kst", j, "c", kt // 4), ("Kst", j, "r"),
                                 ("Qst", j, "c", qc)] +
                          [("Qst", j, "r", qc * 4 + i) for i in range(4)],
                          writes=[("pb", sbank)], tag=f"s{qc}smm")
                    if r >= 0:
                        p.add("vector", lambda e, kt=kt, coloff=coloff,
                              sb_=sb_, sc=sc:
                              e.scalar_tensor_tensor(
                                  sc[:, coloff:512],
                                  sb_[:, coloff:512], -CLIP_HI,
                                  tri[:, 0:512 - coloff], AL.max, AL.min),
                              reads=[("pb", sbank), ("tri",)],
                              writes=[(f"S_scr{eb}", 0), (f"S_scr{eb}", 1)],
                              tag="clipd")
                    else:
                        p.add("vector", lambda e, kt=kt, sb_=sb_, sc=sc:
                              e.tensor_scalar(sc[:, 0:512], sb_[:, 0:512],
                                              CLIP_HI, -CLIP_HI,
                                              AL.min, AL.max),
                              reads=[("pb", sbank)],
                              writes=[(f"S_scr{eb}", 0), (f"S_scr{eb}", 1)],
                              tag="clip")
                def expf(kt):
                    r = kt - 4 * qc
                    coloff = 0 if r < 0 else 128 * r
                    eb = ebufs[kt % 2]
                    sc = S_scr[eb]
                    p.add("scalar", lambda e, kt=kt, coloff=coloff, sc=sc,
                          eb=eb:
                          e.activation(E_T[eb][:, coloff:512],
                                       sc[:, coloff:512], AF.Exp),
                          reads=[(f"S_scr{eb}", 0), (f"S_scr{eb}", 1)],
                          writes=[(f"E_T{eb}",)], tag="exp")

                def av(kt):
                    r = kt - 4 * qc
                    coloff = 0 if r < 0 else 128 * r
                    eb = ebufs[kt % 2]
                    p.add("tensor", lambda e, kt=kt, coloff=coloff, eb=eb:
                          e.matmul(pa[0:65, coloff:512],
                                   ov["V_g"][:, kt * HG * 65 + j * 65:
                                             kt * HG * 65 + j * 65 + 65],
                                   E_T[eb][:, coloff:512],
                                   start=(kt == 0), stop=(kt == nkt - 1)),
                          reads=[("V_g", kt), ("V_g", "ones"),
                                 (f"E_T{eb}",)],
                          writes=[("pb", abank)], tag="av")

                par = abank - 2
                rf, rb, at = rs_f[par], rs_bf[par], attn_tmp[par]

                def epi1():
                    p.add("scalar", lambda e: e.activation(
                        rf[0:1, :], pa[64:65, :], AF.Ln),
                        reads=[("pb", abank)], writes=[("rs_f", par)],
                        small=True, tag="recip")
                    p.add("scalar", lambda e: e.activation(
                        rb[0:1, :], rf[0:1, :], AF.Exp, scale=-1.0),
                        reads=[("rs_f", par)], writes=[("rs_bf", par)],
                        small=True, tag="recipc")
                    p.add("scalar", lambda e: e.copy(
                        at[:], pa[0:64, :]),
                        reads=[("pb", abank)], writes=[("attn_tmp", par)],
                        tag="aev")

                def epi2():
                    h = hh
                    bb = ebank
                    p.add("tensor", lambda e, bb=bb: e.matmul(
                        pb[bb][0:64, :], ones64[:], rb[0:1, :],
                        start=True, stop=True),
                        reads=[("rs_bf", par), ("ones64",)],
                        writes=[("pb", bb)], tag="bcast")
                    p.add("vector", lambda e, h=h, q0=q0, bb=bb, afb=afb:
                          e.tensor_tensor(
                              afb[(h % 2) * 64:(h % 2) * 64 + 64,
                                  (h // 2) * S + q0:
                                  (h // 2) * S + q0 + 512],
                              at[:], pb[bb][0:64, :], AL.mult),
                          reads=[("attn_tmp", par), ("pb", bb)],
                          writes=[(afn, h, qc)] +
                          [(afn, qc * 4 + i) for i in range(4)], tag="anorm")
                return scores, expf, av, (epi1, epi2), nkt

            for qc in range(4):
                for jp in (0, 2):
                    eb2 = 1 if (qc == 3 and jp == 2) else 4
                    sA, xA, aA, eA, nkt = mk_block(jp, qc, (0, 1), (0, 1), 2,
                                                   g=g, ebank=eb2)
                    sB, xB, aB, eB, _ = mk_block(jp + 1, qc, (6, 7), (2, 3),
                                                 3, g=g, ebank=eb2)
                    sA(0)
                    sB(0)
                    sA(1)
                    sB(1)
                    if pend_tail[0] is not None:
                        for f in pend_tail[0]:
                            f()
                        pend_tail[0] = None
                    if pend_epi[0] is not None:
                        pend_epi[0][0][0]()
                        pend_epi[0][1][0]()
                    xA(0)
                    xB(0)
                    xA(1)
                    xB(1)
                    for kt in range(2, nkt):
                        sA(kt)
                        sB(kt)
                        if pend_epi[0] is not None and kt == 3:
                            pend_epi[0][0][1]()
                            pend_epi[0][1][1]()
                            pend_epi[0] = None
                        aA(kt - 2)
                        aB(kt - 2)
                        xA(kt)
                        xB(kt)
                    pend_tail[0] = [
                        lambda f=aA, n=nkt: f(n - 2),
                        lambda f=aB, n=nkt: f(n - 2),
                        lambda f=aA, n=nkt: f(n - 1),
                        lambda f=aB, n=nkt: f(n - 1),
                    ]
                    if pend_epi[0] is not None:
                        for blk in pend_epi[0]:
                            blk[0]()
                            blk[1]()
                        pend_epi[0] = None
                    pend_epi[0] = (eA, eB)

        if pend_tail[0] is not None:
            for f in pend_tail[0]:
                f()
            pend_tail[0] = None
        if pend_epi[0] is not None:
            for blk in pend_epi[0]:
                blk[0]()
                blk[1]()
            pend_epi[0] = None

        pend_cast = []

        def kv_load(e_idx):
            par = e_idx % 2
            kname = "keysb0" if par == 0 else "keysb1"
            vname = "valsb0" if par == 0 else "valsb1"
            if seq > 0:
                p.add("sync", lambda e, e_idx=e_idx, kname=kname: e.dma_start(
                    out=ov[kname][:, 0:4096],
                    in_=kvscr[e_idx][0:4].rearrange("c p n -> p c n")),
                    reads=[("kvscr", e_idx, pp) for pp in range(4)],
                    writes=[(kname, pp) for pp in range(4)],
                    queue="qsync", slot=kname, tag=sq + "kvd")
                p.add("sync", lambda e, e_idx=e_idx, vname=vname: e.dma_start(
                    out=ov[vname][:, 0:4096],
                    in_=kvscr[e_idx][4:8].rearrange("c p n -> p c n")),
                    reads=[("kvscr", e_idx, 4 + pp) for pp in range(4)],
                    writes=[(vname, pp) for pp in range(4)],
                    queue="qsync", slot=vname, tag=sq + "kvd")
                return kname, vname
            def mk(piece):
                lname = "land0"
                land = ov[lname]
                if piece < 4:
                    src3 = keys_d[e_idx].rearrange(
                        "(c p) f -> p c f", p=128)[
                        :, 2 * piece:2 * piece + 2, :]
                    dname, dcol = kname, piece * 1024
                else:
                    src3 = vals_d[e_idx].rearrange(
                        "(c p) n -> p c n", p=128)[:, piece - 4:piece - 3, :]
                    dname, dcol = vname, (piece - 4) * 1024
                ceng = "gpsimd"

                def _in(src3=src3, land=land, lname=lname):
                    p.add("sync", lambda e: e.dma_start(
                        out=land[:, 0:1024], in_=src3),
                        writes=[(lname,)], queue="qsync", slot=lname,
                        tag=sq + "kvd")

                def _cast(e2, land=land, dname=dname, dcol=dcol):
                    return e2.tensor_copy(ov[dname][:, dcol:dcol + 1024],
                                          land[:, 0:1024])

                def _out(e2, e_idx=e_idx, piece=piece, dname=dname,
                         dcol=dcol):
                    return e2.dma_start(out=kvscr[e_idx][piece],
                                        in_=ov[dname][:, dcol:dcol + 1024])
                return (_in, ceng, _cast, _out, lname, dname, piece % 4,
                        e_idx, piece)

            pieces = [mk(piece) for piece in range(8)]
            pieces[0][0]()
            for k in range(8):
                nxt = pieces[k + 1][0] if k + 1 < 8 else None
                pend_cast.append(pieces[k] + (nxt,))
            return kname, vname

        def drip_cast(n=1):
            for _ in range(min(n, len(pend_cast))):
                (_in, ceng, fn, outfn, lname, dname, pp, e_idx, piece,
                 next_in) = pend_cast.pop(0)
                p.add(ceng, fn, reads=[(lname,)], writes=[(dname, pp)],
                      tag=sq + "kvc")
                p.add("gpsimd", outfn, reads=[(dname, pp)],
                      writes=[("kvscr", e_idx, piece), ("scro", piece % 2)],
                      queue="qgp", slot="scro" + str(piece % 2),
                      tag=sq + "kvso")
                if next_in is not None:
                    next_in()

        load_weights_dma(("wo",), sq + "wo")
        kv_names = {0: kv_load(0)}
        for t in range(NT):
            xr = xrot[t % 2]
            p.add("sync", lambda e, t=t, xr=xr, seq=seq: e.dma_start(
                out=xr[:], in_=x_d[seq][t * 128:(t + 1) * 128, :]),
                writes=[(f"xrot{t % 2}",)], queue="qsync", tag=sq + f"xl2_{t}")
            for half in range(2):
                for ch in range(8):
                    p.add("tensor", lambda e, t=t, half=half, ch=ch, afb=afb:
                          e.matmul(pb[half][:, :],
                                   afb[:, ch * S + t * 128:
                                       ch * S + (t + 1) * 128],
                                   ov["wo"][:, ch * D + half * 512:
                                            ch * D + half * 512 + 512],
                                   start=(ch == 0), stop=(ch == 7)),
                          reads=[("wo",), (afn, 2 * ch, t // 4),
                                 (afn, 2 * ch + 1, t // 4)],
                          writes=[("pb", half)], tag=sq + "omm")
                p.add("vector", lambda e, t=t, half=half, xr=xr:
                      e.tensor_tensor(
                          ov["xp"][:, t * D + half * 512:
                                   t * D + half * 512 + 512],
                          pb[half][:, :], xr[:, half * 512:half * 512 + 512],
                          AL.add),
                      reads=[("pb", half), (f"xrot{t % 2}",)],
                      writes=[("xp", t, half)], tag=sq + "oadd")
            if t >= 2:
                ln_stage2(t - 2, sq + f"D{t - 2}_", xtn, xtb, part="tr")
                ln_stage2(t - 2, sq + f"D{t - 2}_", xtn, xtb, part="ev")
            srcA = ov["xp"].ap()[:, t * D:t * D + 512]
            srcB = ov["xp"].ap()[:, t * D + 512:t * D + 1024]
            ln_stats(srcA, srcB, [("xp", t, 0)], [("xp", t, 1)],
                     t, sq + f"D{t}_")
            ln_norm(srcA, srcB, [("xp", t, 0)], [("xp", t, 1)],
                    t, sq + f"D{t}_")
            if t == NT - 1:
                ln_stage2(t - 1, sq + f"D{t - 1}_", xtn, xtb)
                ln_stage2(t, sq + f"D{t}_", xtn, xtb)
            def gates_tile(t):
                for xc in range(8):
                    p.add("tensor", lambda e, t=t, xc=xc, xtb=xtb: e.matmul(
                        pb[3][:, 0:32],
                        xtb[:, xc * S + t * 128:xc * S + (t + 1) * 128],
                        selT_bf[:, xc * 32:(xc + 1) * 32],
                        start=(xc == 0), stop=(xc == 7)),
                        reads=[(xtn, t), ("selT_bf",)], writes=[("pb", 3)],
                        tag=sq + "gmm")
                p.add("scalar", lambda e, t=t: e.activation(
                    g_work[:, t * NE:(t + 1) * NE], pb[3][:, 0:32], AF.Exp,
                    scale=-1.0),
                    reads=[("pb", 3)], writes=[("g_work", "sig", t)],
                    tag=sq + "gsig")
                p.add("vector", lambda e, t=t: e.tensor_scalar(
                    g_ind[:, t * NE:(t + 1) * NE],
                    g_work[:, t * NE:(t + 1) * NE], 1.0, None, AL.add),
                    reads=[("g_work", "sig", t)], writes=[("g_ind", "sig", t)],
                    small=True, tag=sq + "gsig2")
                p.add("vector", lambda e, t=t: e.reciprocal(
                    gates_all[:, t * NE:(t + 1) * NE],
                    g_ind[:, t * NE:(t + 1) * NE]),
                    reads=[("g_ind", "sig", t)], writes=[("gates_all", t)],
                    small=True, tag=sq + "gsig3")
            if t >= 2:
                gates_tile(t - 2)
            if t == NT - 1:
                gates_tile(t - 1)
                gates_tile(t)
            drip_cast(1)
        kv_names[1] = kv_load(1)

        g3 = g_work.ap().rearrange("p (t e) -> p t e", e=NE)
        ga3 = gates_all.ap().rearrange("p (t e) -> p t e", e=NE)
        gi3 = g_ind.ap().rearrange("p (t e) -> p t e", e=NE)
        gr = [("gates_all", t) for t in range(NT)]
        p.add("vector", lambda e: e.tensor_copy(g_work[:], gates_all[:]),
              reads=gr, writes=[("g_work",)], tag=sq + "t4cp")
        for rnd in range(4):
            m2 = g_m.ap()[:, rnd * NT:(rnd + 1) * NT].rearrange(
                "p (t o) -> p t o", o=1)
            p.add("vector", lambda e, m2=m2: e.tensor_reduce(
                m2, g3, AX, AL.max),
                reads=[("g_work",)], writes=[("g_m", rnd)], small=True,
                tag=sq + f"t4m{rnd}")
            if rnd < 3:
                mb = m2.broadcast_to((128, NT, NE))
                p.add("vector", lambda e, mb=mb: e.tensor_tensor(
                    gi3, g3, mb, AL.is_ge),
                    reads=[("g_work",), ("g_m", rnd)], writes=[("g_ind",)],
                    tag=sq + f"t4i{rnd}")
                p.add("vector", lambda e: e.scalar_tensor_tensor(
                    g3, gi3, -1e9, g3, AL.mult, AL.add),
                    reads=[("g_ind",), ("g_work",)], writes=[("g_work",)],
                    tag=sq + f"t4u{rnd}")
        m4 = g_m.ap()[:, 3 * NT:4 * NT].rearrange("p (t o) -> p t o", o=1)
        m4b = m4.broadcast_to((128, NT, NE))
        p.add("vector", lambda e: e.tensor_tensor(gi3, ga3, m4b, AL.is_ge),
              reads=gr + [("g_m", 3)], writes=[("g_ind",)], tag=sq + "t4fi")
        p.add("vector", lambda e: e.tensor_tensor(
            gate_full[:], gates_all[:], g_ind[:], AL.mult),
            reads=gr + [("g_ind",)],
            writes=[("gate_full", t) for t in range(NT)], tag=sq + "t4gf")

        if seq + 1 < nseq:
            wl = emit_A_chunks(seq + 1, woven=True)
            woven.add(seq + 1)
        else:
            wl = []
        drip_cast(8)
        for e_idx in range(NE):
            kname, vname = kv_names[e_idx]
            if e_idx + 1 < NE and e_idx + 1 not in kv_names:
                kv_names[e_idx + 1] = kv_load(e_idx + 1)
            kdst, vdst = ov[kname], ov[vname]

            def h_stage(grp, e_idx=e_idx, kname=kname, kdst=kdst):
                kreads = [(kname, pc) for pc in range(4)]
                for fc in range(4):
                    for xc in range(8):
                        p.add("tensor", lambda e, grp=grp, fc=fc, xc=xc, xtb=xtb:
                              e.matmul(pb[fc][:, :],
                                       kdst[:, xc * EF + fc * 128:
                                            xc * EF + fc * 128 + 128],
                                       xtb[:, xc * S + grp * 512:
                                           xc * S + (grp + 1) * 512],
                                       start=(xc == 0), stop=(xc == 7)),
                              reads=[(xtn, 4 * grp), (xtn, 4 * grp + 1),
                                     (xtn, 4 * grp + 2),
                                     (xtn, 4 * grp + 3)] + kreads,
                              writes=[("pb", fc)], tag=sq + "hmm")
                hname = "h_sb0" if grp % 2 == 0 else "h_sb1"
                hsb = ov[hname]
                for fc in range(4):
                    p.add("scalar", lambda e, fc=fc, hsb=hsb: e.activation(
                        hsb[:, fc * 512:(fc + 1) * 512], pb[fc][:, :],
                        AF.Relu),
                        reads=[("pb", fc)], writes=[(hname, fc)],
                        tag=sq + "hrelu")
                return hname, hsb

            def y_stage(grp, hname, hsb, vname=vname, vdst=vdst,
                        e_idx=e_idx):
                vreads = [(vname, pc) for pc in range(4)]
                hreads = [(hname, fc) for fc in range(4)]
                YB = (4, 6, 7)
                for tl in range(4):
                    t = grp * 4 + tl
                    for half in range(2):
                        yb = YB[(tl * 2 + half) % 3]
                        for fc in range(4):
                            p.add("tensor", lambda e, fc=fc, half=half,
                                  hsb=hsb, tl=tl, yb=yb:
                                  e.matmul(pb[yb][:, :],
                                           hsb[:, fc * 512 + tl * 128:
                                               fc * 512 + tl * 128 + 128],
                                           vdst[:, fc * D + half * 512:
                                                fc * D + half * 512 + 512],
                                           start=(fc == 0), stop=(fc == 3)),
                                  reads=hreads + vreads,
                                  writes=[("pb", yb)], tag=sq + "ymm")
                        p.add("vector", lambda e, t=t, half=half,
                              e_idx=e_idx, yb=yb:
                              e.scalar_tensor_tensor(
                                  ov["xp"][:, t * D + half * 512:
                                           t * D + half * 512 + 512],
                                  pb[yb][:, :],
                                  gate_full.ap()[:, t * NE + e_idx:
                                                 t * NE + e_idx + 1],
                                  ov["xp"][:, t * D + half * 512:
                                           t * D + half * 512 + 512],
                                  AL.mult, AL.add),
                              reads=[("pb", yb), ("xp", t, half),
                                     ("gate_full", t)],
                              writes=[("xp", t, half)], tag=sq + "yadd")

            prev = None
            for grp in range(4):
                drip_cast(4 if grp < 2 else 0)
                if wl:
                    wl.pop(0)()
                cur = (grp,) + h_stage(grp)
                if prev is not None:
                    y_stage(*prev)
                prev = cur
            y_stage(*prev)

        while wl:
            wl.pop(0)()

        for t in range(NT):
            p.add("scalar", lambda e, t=t, seq=seq: e.dma_start(
                out=y_d[seq][t * 128:(t + 1) * 128, :],
                in_=ov["xp"][:, t * D:(t + 1) * D]),
                reads=[("xp", t, 0), ("xp", t, 1)], queue="qsc",
                writes=[("yst", t % 4)],
                slot=f"st{t % 4}", must_inc=True, tag=sq + f"store{t}")

    p.finalize_and_emit(nc)
    return nc


def kernel(L0_attn_x, L0_attn_W_DQ, L0_attn_W_UQ, L0_attn_W_QR, L0_attn_W_DKV,
           L0_attn_W_UK, L0_attn_W_UV, L0_attn_W_KR, L0_attn_W_O,
           L0_ffn_expert_sel, L0_ffn_keys, L0_ffn_values):
    x = np.ascontiguousarray(np.asarray(L0_attn_x))
    common = {
        "wdq": np.ascontiguousarray(L0_attn_W_DQ),
        "wuq": np.ascontiguousarray(L0_attn_W_UQ),
        "wqr": _perm_qr(L0_attn_W_QR),
        "wdkv": np.ascontiguousarray(L0_attn_W_DKV),
        "wuk": np.ascontiguousarray(L0_attn_W_UK),
        "wuv": np.ascontiguousarray(L0_attn_W_UV),
        "wkr": _perm_qr(L0_attn_W_KR),
        "wo": np.ascontiguousarray(L0_attn_W_O),
        "selT": np.ascontiguousarray(np.asarray(L0_ffn_expert_sel).T),
        "keys": np.ascontiguousarray(L0_ffn_keys),
        "vals": np.ascontiguousarray(L0_ffn_values),
    }
    nc = build_program()
    in_maps = [dict(common, x=np.ascontiguousarray(x[c * BPC:(c + 1) * BPC]))
               for c in range(NCORES)]
    res = run_bass_kernel_spmd(nc, in_maps, core_ids=list(range(NCORES)))
    return np.concatenate([r["y"] for r in res.results], axis=0)
